# version 09
# baseline (speedup 1.0000x reference)
import numpy as np
import ml_dtypes
import concourse.bass as bass
import concourse.mybir as mybir
from concourse.bass_utils import run_bass_kernel_spmd

F32 = mybir.dt.float32
BF16 = mybir.dt.bfloat16
AF = mybir.ActivationFunctionType
ALU = mybir.AluOpType
AX = mybir.AxisListType

S = 2048
D = 1024
NCH = 8
NH = 16
HD = 64
KF = 4
NSLOT = 512
LR = 64
NE = 8
FF = 4096
EPS = 1e-5
INV16 = 1.0 / 16.0

SB_BASE = 17472

ENG_SEM = {"sync": "q_sy", "gpsimd": "q_gp", "vector": "e_v", "scalar": "e_s",
           "tensor": "e_t"}
COMPUTE_SEM = {"sync": "e_sy", "gpsimd": "e_p", "vector": "e_v", "scalar": "e_s",
               "tensor": "e_t"}
DMA_POOL = 8
SEM_KEYS = (["e_v", "e_s", "e_t", "e_p", "e_sy"]
            + [f"q_sy{i}" for i in range(DMA_POOL)]
            + [f"q_gp{i}" for i in range(DMA_POOL)])


class Em:
    def __init__(self, nc):
        self.nc = nc
        self.ops = []
        self.sb_base = {}
        self.psum_id = {}
        self.writes = []
        self.reads = []

    def reg_sb(self, t):
        self.sb_base[t.name] = t.manual_sbuf_range[0]
        return t

    def reg_ps(self, t):
        self.psum_id[t.name] = len(self.psum_id)
        return t

    def _box(self, ap):
        t = ap.tensor
        tn = t.name
        if tn in self.psum_id:
            i = self.psum_id[tn]
            return (0, 128, 1 << 30 | i << 16, 1 << 30 | (i << 16) + 1)
        if tn not in self.sb_base:
            return None
        base = self.sb_base[tn]
        esz = 2 if ap.dtype == BF16 else 4
        shp = t.shape
        fbytes = 1
        for s in shp[1:]:
            fbytes *= s
        fbytes *= (2 if t.dtype == BF16 else 4)
        p0 = ap.base_partition()
        pn = ap.partition_size()
        off_b = int(ap.offset) * esz
        c0 = off_b - p0 * fbytes
        ext = 1
        pairs = [list(p) for p in ap.ap]
        for st, n in pairs[1:]:
            ext += (n - 1) * abs(st)
        ext *= esz
        return (p0, p0 + pn, base + c0, base + c0 + ext)

    @staticmethod
    def _ov(a, b):
        return a[0] < b[1] and b[0] < a[1] and a[2] < b[3] and b[2] < a[3]

    def op(self, eng, fn, reads=(), writes=(), dma=False, tag=""):
        idx = len(self.ops)
        rb = [b for b in (self._box(a) for a in reads) if b is not None]
        wb = [b for b in (self._box(a) for a in writes) if b is not None]
        deps = set()
        for b in rb:
            for e in self.writes:
                if self._ov(b, e[0]):
                    deps.add(e[1])
        for b in wb:
            for e in self.writes:
                if self._ov(b, e[0]):
                    deps.add(e[1])
            for e in self.reads:
                if self._ov(b, e[0]):
                    deps.add(e[1])
        qidx = None
        if dma:
            if not hasattr(self, "qcount"):
                self.qcount = {}
            qidx = self.qcount.get(eng, 0)
            self.qcount[eng] = qidx + 1
        self.ops.append(dict(eng=eng, fn=fn, dma=dma, deps=deps, tag=tag,
                             idx=idx, need_inc=False, qidx=qidx))
        for b in rb:
            for e in self.reads:
                if e[0] == b and self.ops[e[1]]["eng"] == eng and \
                   self.ops[e[1]]["dma"] == dma:
                    e[1] = idx
                    break
            else:
                self.reads.append([b, idx])
        for b in wb:
            for e in self.writes:
                if e[0] == b:
                    e[1] = idx
                    break
            else:
                self.writes.append([b, idx])
        return idx

    def build(self, block, sems, strict=()):
        strict = set(strict)
        ops = self.ops
        n = len(ops)
        for o in ops:
            drop = []
            for d in o["deps"]:
                dop = ops[d]
                if dop["eng"] == o["eng"] and not dop["dma"] and not o["dma"] \
                        and o["eng"] not in strict:
                    drop.append(d)
            for d in drop:
                o["deps"].discard(d)
            for d in o["deps"]:
                ops[d]["need_inc"] = True
        for o in ops:
            if o["dma"]:
                o["need_inc"] = True
        counters = {}
        prev_pool_op = {}
        for o in ops:
            if o["dma"]:
                key = ENG_SEM[o["eng"]] + str(o["qidx"] % DMA_POOL)
                rnd = o["qidx"] // DMA_POOL
                o["inc_val"] = 16 * (rnd + 1)
                counters[key] = o["inc_val"]
                o["self_wait"] = (key, 16 * rnd, prev_pool_op.get(key)) \
                    if rnd > 0 else None
                prev_pool_op[key] = o["idx"]
            else:
                key = COMPUTE_SEM[o["eng"]]
                if o["need_inc"]:
                    counters[key] = counters.get(key, 0) + 1
                    o["inc_val"] = counters[key]
                else:
                    o["inc_val"] = None
                o["self_wait"] = None
            o["sem_key"] = key
        self.totals = dict(counters)
        eng_clock = {e: {} for e in ENG_SEM}
        op_vc = [None] * n
        for o in ops:
            e = o["eng"]
            clk = eng_clock[e]
            waits = []
            if o["self_wait"] is not None:
                sk, tv, pidx = o["self_wait"]
                if clk.get(sk, 0) < tv:
                    waits.append((sk, tv))
                    if pidx is not None:
                        for k, v in op_vc[pidx].items():
                            if clk.get(k, 0) < v:
                                clk[k] = v
                    if clk.get(sk, 0) < tv:
                        clk[sk] = tv
            for d in sorted(o["deps"]):
                dop = ops[d]
                sk = dop["sem_key"]
                tv = dop["inc_val"]
                assert tv is not None
                if clk.get(sk, 0) < tv:
                    waits.append((sk, tv))
                    for k, v in op_vc[d].items():
                        if clk.get(k, 0) < v:
                            clk[k] = v
                    if clk.get(sk, 0) < tv:
                        clk[sk] = tv
            wd = {}
            for sk, tv in waits:
                wd[sk] = max(wd.get(sk, 0), tv)
            o["waits"] = list(wd.items())
            vc = dict(clk)
            if o["inc_val"] is not None:
                vc[o["sem_key"]] = o["inc_val"]
            op_vc[o["idx"]] = vc
        for eng_name in ("sync", "gpsimd", "vector", "scalar", "tensor"):
            myops = [o for o in ops if o["eng"] == eng_name]
            if not myops and eng_name != "sync":
                continue

            def body(e, myops=myops, eng_name=eng_name):
                for o in myops:
                    for sk, tv in o["waits"]:
                        e.wait_ge(sems[sk], tv)
                    inst = o["fn"](e)
                    if o["inc_val"] is not None:
                        inst.then_inc(sems[o["sem_key"]],
                                      16 if o["dma"] else 1)
                if eng_name == "sync":
                    for qk, tv in sorted(self.totals.items()):
                        if qk.startswith("q_"):
                            e.wait_ge(sems[qk], tv)
            getattr(block, eng_name)(body)


def make_consts():
    H4 = np.array([[1, 1, 1, 1], [1, -1, 1, -1], [1, 1, -1, -1], [1, -1, -1, 1]],
                  np.float64)
    inv_freq = 1.0 / 10000.0 ** (np.arange(0, HD, 2, dtype=np.float64) / HD)
    pos = np.arange(S, dtype=np.float64)
    freqs = pos[:, None] * inv_freq[None, :]
    emb = np.concatenate([freqs, freqs], axis=-1)
    cos, sin = np.cos(emb), np.sin(emb)
    phase = H4[np.arange(S) % 4][:, np.arange(HD) % 4]
    cph64 = (cos * phase).T
    sph64 = (sin * phase).T
    cph = np.concatenate([cph64, cph64], axis=0)
    sph = np.concatenate([sph64, sph64], axis=0)
    p = np.arange(128)
    phb = H4[:, p % 4].T
    r = np.arange(128)
    dd = np.arange(HD)
    ph2t = H4[r % 4][:, dd % 4]
    j32 = np.arange(32)
    sel = (j32[None, :] == (r[:, None] // 4)).astype(np.float64)
    seli = 1.0 - sel
    bneg = np.where(j32[None, :] > (r[:, None] // 4), -1e30, 0.0)
    pblk = np.zeros((128, 128))
    for a in (0, 64):
        for dp in range(32):
            pblk[a + dp + 32, a + dp] = -1.0
        for dp in range(32, 64):
            pblk[a + dp - 32, a + dp] = 1.0
    b16 = ml_dtypes.bfloat16
    return {
        "c_cph": cph.astype(b16), "c_sph": sph.astype(b16),
        "c_phb": phb.astype(b16), "c_ph2t": ph2t.astype(np.float32),
        "c_sel": sel.astype(np.float32), "c_seli": seli.astype(np.float32),
        "c_bneg": bneg.astype(np.float32),
        "c_selb": sel.astype(b16), "c_selu8": sel.astype(np.uint8),
        "c_idb": np.eye(128, dtype=b16), "c_idf": np.eye(128, dtype=np.float32),
        "c_pblk": pblk.astype(b16),
    }


def build_program(taps=(), strict=()):
    nc = bass.Bass("TRN2", target_bir_lowering=False)
    taps = set(taps)

    X = nc.dram_tensor("x", [S, D], F32, kind="ExternalInput").ap()
    Y = nc.dram_tensor("y", [S, D], F32, kind="ExternalOutput").ap()
    WQ = nc.dram_tensor("wq", [D, D], F32, kind="ExternalInput").ap()
    WK = nc.dram_tensor("wk", [D, D], F32, kind="ExternalInput").ap()
    WV = nc.dram_tensor("wv", [D, D], F32, kind="ExternalInput").ap()
    WO = nc.dram_tensor("wo", [D, D], F32, kind="ExternalInput").ap()
    AQ = nc.dram_tensor("aq", [D, LR], F32, kind="ExternalInput").ap()
    BQ = nc.dram_tensor("bq", [LR, D], F32, kind="ExternalInput").ap()
    AV = nc.dram_tensor("av", [D, LR], F32, kind="ExternalInput").ap()
    BV = nc.dram_tensor("bv", [LR, D], F32, kind="ExternalInput").ap()
    AO = nc.dram_tensor("ao", [D, LR], F32, kind="ExternalInput").ap()
    BO = nc.dram_tensor("bo", [LR, D], F32, kind="ExternalInput").ap()
    WG = nc.dram_tensor("wg", [NE, D, FF], F32, kind="ExternalInput").ap()
    WU = nc.dram_tensor("wu", [NE, D, FF], F32, kind="ExternalInput").ap()
    WD_ = nc.dram_tensor("wd", [NE, FF, D], F32, kind="ExternalInput").ap()
    CONST_SPECS = {
        "c_cph": ([128, S], BF16), "c_sph": ([128, S], BF16),
        "c_phb": ([128, 4], BF16), "c_ph2t": ([128, HD], F32),
        "c_sel": ([128, 32], F32), "c_seli": ([128, 32], F32),
        "c_bneg": ([128, 32], F32), "c_selb": ([128, 32], BF16),
        "c_selu8": ([128, 32], mybir.dt.uint8),
        "c_idb": ([128, 128], BF16),
        "c_idf": ([128, 128], F32), "c_pblk": ([128, 128], BF16),
    }
    CD = {k: nc.dram_tensor(k, sh, dt, kind="ExternalInput").ap()
          for k, (sh, dt) in CONST_SPECS.items()}
    tap_d = {}
    for name, sh, dt in (
        ("xn1", [128, NCH * S], F32), ("outfm", [128, NCH * S], F32),
        ("ybf", [128, NCH * S], F32), ("x1", [128, NCH * S], F32),
        ("acc", [128, NCH * 1024], F32), ("aq0", [128, S], F32),
        ("fullk0", [128, NSLOT], F32), ("vsm0", [128, NSLOT], F32),
        ("diff0", [128, S], F32), ("csp0", [128, 32], F32),
        ("kshb0", [128, 512], F32), ("pkb0", [128, 512], F32),
        ("prb0", [128, 512], F32),
    ):
        if name in taps:
            tap_d[name] = nc.dram_tensor("tap_" + name, sh, dt,
                                         kind="ExternalOutput").ap()

    em = Em(nc)
    A = nc.alloc_sbuf_tensor_at

    xf = em.reg_sb(A("xf", [128, NCH * S], F32, offset=SB_BASE + 0))
    xnb = em.reg_sb(A("xnb", [128, NCH * S], BF16, offset=SB_BASE + 65536))
    ar1 = em.reg_sb(A("ar1", [128, NCH * S], BF16, offset=SB_BASE + 98304))
    ar2 = em.reg_sb(A("ar2", [128, 12288], BF16, offset=SB_BASE + 131072))
    accf = em.reg_sb(A("accf", [128, 8192], F32, offset=SB_BASE + 155648))
    ef = em.reg_sb(A("ef", [128, 512], F32, offset=SB_BASE + 155648))
    s1 = em.reg_sb(A("s1", [128, 512], F32, offset=SB_BASE + 157696))
    s2 = em.reg_sb(A("s2", [128, 512], F32, offset=SB_BASE + 159744))
    s3 = em.reg_sb(A("s3", [128, 512], F32, offset=SB_BASE + 161792))
    s4 = em.reg_sb(A("s4", [128, 512], F32, offset=SB_BASE + 163840))
    s32 = em.reg_sb(A("s32", [128, 32], F32, offset=SB_BASE + 165888))
    s64a = em.reg_sb(A("s64a", [128, 64], F32, offset=SB_BASE + 166016))
    s64b = em.reg_sb(A("s64b", [128, 64], F32, offset=SB_BASE + 166272))
    rows6 = em.reg_sb(A("rows6", [128, 8], F32, offset=SB_BASE + 166528))
    csp = em.reg_sb(A("csp", [128, 32], F32, offset=SB_BASE + 166560))
    qb = em.reg_sb(A("qb", [128, 512], BF16, offset=SB_BASE + 166912))
    rb = em.reg_sb(A("rb", [128, 512], BF16, offset=SB_BASE + 167936))
    kb = em.reg_sb(A("kb", [128, 512], BF16, offset=SB_BASE + 168960))
    kshb = em.reg_sb(A("kshb", [128, 512], BF16, offset=SB_BASE + 169984))
    vb = em.reg_sb(A("vb", [128, 512], BF16, offset=SB_BASE + 171008))
    vshb = em.reg_sb(A("vshb", [128, 512], BF16, offset=SB_BASE + 172032))
    pkb = em.reg_sb(A("pkb", [128, 512], BF16, offset=SB_BASE + 173056))
    pvb = em.reg_sb(A("pvb", [128, 512], BF16, offset=SB_BASE + 174080))
    prb = em.reg_sb(A("prb", [128, 512], BF16, offset=SB_BASE + 175104))
    otm = em.reg_sb(A("otm", [128, 128], BF16, offset=SB_BASE + 176128))
    aab = em.reg_sb(A("aab", [128, 1536], BF16, offset=SB_BASE + 176384))
    bbb2 = em.reg_sb(A("bbb2", [128, 2048], BF16, offset=SB_BASE + 179456))
    lob = em.reg_sb(A("lob", [64, 2048], BF16, offset=SB_BASE + 183552))
    stage = em.reg_sb(A("stage", [128, 1024], F32, offset=SB_BASE + 155648))
    stage2 = em.reg_sb(A("stage2", [128, 1024], F32, offset=SB_BASE + 159744))
    fs = em.reg_sb(A("fs", [128, 2048], F32, offset=SB_BASE + 188416))
    rowsA = em.reg_sb(A("rowsA", [1, 1024], F32, offset=SB_BASE + 196608))
    cph = em.reg_sb(A("cph", [128, S], BF16, offset=SB_BASE + 200704))
    sph = em.reg_sb(A("sph", [128, S], BF16, offset=SB_BASE + 204800))
    ph2t = em.reg_sb(A("ph2t", [128, HD], F32, offset=SB_BASE + 208896))
    sel = em.reg_sb(A("sel", [128, 32], F32, offset=SB_BASE + 209152))
    seli = em.reg_sb(A("seli", [128, 32], F32, offset=SB_BASE + 209280))
    bneg = em.reg_sb(A("bneg", [128, 32], F32, offset=SB_BASE + 209408))
    idb = em.reg_sb(A("idb", [128, 128], BF16, offset=SB_BASE + 209536))
    idf = em.reg_sb(A("idf", [128, 128], F32, offset=SB_BASE + 209792))
    pblk = em.reg_sb(A("pblk", [128, 128], BF16, offset=SB_BASE + 210304))
    phb = em.reg_sb(A("phb", [128, 4], BF16, offset=SB_BASE + 210560))
    epsc = em.reg_sb(A("epsc", [128, 1], F32, offset=SB_BASE + 210592))
    ones1f = em.reg_sb(A("ones1f", [128, 1], F32, offset=SB_BASE + 210624))
    ones1b = em.reg_sb(A("ones1b", [128, 1], BF16, offset=SB_BASE + 210656))
    ones2f = em.reg_sb(A("ones2f", [1, 128], F32, offset=SB_BASE + 210688))
    selb = em.reg_sb(A("selb", [128, 32], BF16, offset=SB_BASE + 211200))
    selu8 = em.reg_sb(A("selu8", [128, 32], mybir.dt.uint8, offset=SB_BASE + 211264))

    ps0 = em.reg_ps(nc.alloc_psum_tensor("ps0", [128, 512], F32))
    ps1 = em.reg_ps(nc.alloc_psum_tensor("ps1", [128, 512], F32))
    ps2 = em.reg_ps(nc.alloc_psum_tensor("ps2", [128, 512], F32))
    ps3 = em.reg_ps(nc.alloc_psum_tensor("ps3", [128, 512], F32))
    ps4 = em.reg_ps(nc.alloc_psum_tensor("ps4", [128, 512], F32))
    ps5 = em.reg_ps(nc.alloc_psum_tensor("ps5", [128, 1024], BF16))
    ps6 = em.reg_ps(nc.alloc_psum_tensor("ps6", [128, 512], F32))
    ps7 = em.reg_ps(nc.alloc_psum_tensor("ps7", [128, 1024], BF16))

    SB = {}
    sems = {}

    def sap(t, p0, np_, c0, n, stride=1):
        F = t.shape[1]
        return bass.AP(t, p0 * F + c0, [[F, np_], [stride, n]])

    def mm(out_ap, lhsT_ap, rhs_ap, start, stop, tag="", rd=None):
        em.op("tensor",
              (lambda e, o=out_ap, l=lhsT_ap, r=rhs_ap, st=start, sp=stop:
               e.matmul(o, l, r, start=st, stop=sp)),
              reads=(rd if rd is not None else [lhsT_ap, rhs_ap]),
              writes=[out_ap], tag=tag)

    def tp(out_ap, in_ap, id_ap, tag=""):
        em.op("tensor",
              (lambda e, o=out_ap, i=in_ap, d=id_ap: e.transpose(o, i, d)),
              reads=[in_ap, id_ap], writes=[out_ap], tag=tag)

    def vop(fn, outs, ins, eng="vector", tag=""):
        em.op(eng, fn, reads=ins, writes=outs, tag=tag)

    def dma(eng, out_ap, in_ap, tag=""):
        em.op(eng, (lambda e, o=out_ap, i=in_ap: e.dma_start(out=o, in_=i)),
              reads=[in_ap], writes=[out_ap], dma=True, tag=tag)

    def tap(name, src_ap):
        if name in tap_d:
            dma("gpsimd", tap_d[name], src_ap, tag="tap_" + name)

    def xfc(c, t0, n):
        return xf[:, c * S + t0:c * S + t0 + n]

    def xnbc(c, t0, n):
        return xnb[:, c * S + t0:c * S + t0 + n]

    def ar1c(c, t0, n):
        return ar1[:, c * S + t0:c * S + t0 + n]

    for name, t in (("c_cph", cph), ("c_sph", sph), ("c_phb", phb),
                    ("c_ph2t", ph2t), ("c_sel", sel), ("c_seli", seli),
                    ("c_bneg", bneg), ("c_selb", selb), ("c_selu8", selu8), ("c_idb", idb),
                    ("c_idf", idf),
                    ("c_pblk", pblk)):
        dma("sync", t[tuple(slice(0, s) for s in t.shape)], CD[name], tag="const")
    vop(lambda e: e.memset(ones1f[:, :], 1.0), [ones1f[:, :]], [])
    vop(lambda e: e.memset(ones1b[:, :], 1.0), [ones1b[:, :]], [])
    vop(lambda e: e.memset(ones2f[:, :], 1.0), [ones2f[:, :]], [])
    vop(lambda e: e.memset(epsc[:, :], EPS), [epsc[:, :]], [])

    for ti in range(16):
        stg = stage if ti % 2 == 0 else stage2
        dma("sync", stg[:, :], X[ti * 128:(ti + 1) * 128, :], tag="in")
        for c in range(NCH):
            pst = ps4 if c % 2 == 0 else ps6
            tp(pst[:, 0:128], stg[:, c * 128:(c + 1) * 128], idf[:, :], tag="io_tp")
            vop((lambda e, o=xfc(c, ti * 128, 128), i=pst[:, 0:128]:
                 e.tensor_copy(o, i)),
                [xfc(c, ti * 128, 128)], [pst[:, 0:128]], tag="io_ev")

    def ln_stats(src_fn, src_dtype, t0, n, sqscr=None):
        o1 = ones1f if src_dtype == F32 else ones1b
        sqa, sqb = sqscr if sqscr is not None else (s3, s4)
        for c in range(NCH):
            sc = src_fn(c)
            mm(ps4[0:1, 0:n], o1[:, :], sc, c == 0, c == NCH - 1, tag="ln_sum")
            sq = (sqa if c % 2 == 0 else sqb)[:, 0:n]
            vop((lambda e, o=sq, i=sc: e.activation(o, i, AF.Square)),
                [sq], [sc], eng="scalar", tag="ln_sq")
            mm(ps6[0:1, 0:n], ones1f[:, :], sq, c == 0, c == NCH - 1, tag="ln_sum2")
        vop((lambda e, n=n: e.tensor_scalar_mul(ps4[0:1, 0:n], ps4[0:1, 0:n],
                                                1.0 / D)),
            [ps4[0:1, 0:n]], [ps4[0:1, 0:n]], tag="ln_r1")
        vop((lambda e, n=n: e.tensor_scalar_mul(ps6[0:1, 0:n], ps6[0:1, 0:n],
                                                1.0 / D)),
            [ps6[0:1, 0:n]], [ps6[0:1, 0:n]], tag="ln_r2")
        msq = rowsA[0:1, 0:512]
        vop((lambda e, o=msq, n=n: e.activation(o[:, 0:n], ps4[0:1, 0:n],
                                                AF.Square)),
            [msq], [ps4[0:1, 0:n]], eng="scalar", tag="ln_r3")
        vop((lambda e, n=n, m=msq: e.tensor_sub(ps6[0:1, 0:n], ps6[0:1, 0:n],
                                                m[:, 0:n])),
            [ps6[0:1, 0:n]], [ps6[0:1, 0:n], msq], tag="ln_r4")
        rstd = rowsA[0:1, 512:1024]
        vop((lambda e, o=rstd, n=n: e.activation(o[:, 0:n], ps6[0:1, 0:n],
                                                 AF.Sqrt, bias=epsc[0:1, 0:1])),
            [rstd], [ps6[0:1, 0:n], epsc[0:1, 0:1]], eng="scalar", tag="ln_r5")
        vop((lambda e, o=rstd, n=n: e.reciprocal(o[:, 0:n], o[:, 0:n])),
            [rstd], [rstd], tag="ln_r6")
        mr = rowsA[0:1, 0:512]
        vop((lambda e, o=mr, r=rstd, n=n: e.tensor_mul(o[:, 0:n], ps4[0:1, 0:n],
                                                       r[:, 0:n])),
            [mr], [ps4[0:1, 0:n], rstd], tag="ln_r7")
        mm(ps2[:, 0:n], ones2f[:, :], mr[:, 0:n], True, True, tag="ln_bc1")
        mm(ps3[:, 0:n], ones2f[:, :], rstd[:, 0:n], True, True, tag="ln_bc2")

    def ln_to_bf(src_fn, dst_fn, t0, n, scr):
        for c in range(NCH):
            sc = src_fn(c)
            dc = dst_fn(c)
            sA = scr[:, 0:n]
            vop((lambda e, o=sA, i=sc, n=n: e.tensor_mul(o, i, ps3[:, 0:n])),
                [sA], [sc, ps3[:, 0:n]], tag="ln_n1")
            vop((lambda e, o=dc, i=sA, n=n: e.tensor_sub(o, i, ps2[:, 0:n])),
                [dc], [sA, ps2[:, 0:n]], tag="ln_n2")

    def ln_res_add(src_fn, src_dtype, dst_fn, t0, n, scrA, scrB, scrC):
        for c in range(NCH):
            sc = src_fn(c)
            dc = dst_fn(c)
            if src_dtype == BF16:
                sf = scrC[:, 0:n]
                vop((lambda e, o=sf, i=sc: e.tensor_copy(o, i)),
                    [sf], [sc], tag="ln_a0")
            else:
                sf = sc
            a = scrA[:, 0:n]
            vop((lambda e, o=a, i=sf, n=n: e.tensor_mul(o, i, ps3[:, 0:n])),
                [a], [sf, ps3[:, 0:n]], tag="ln_a1")
            b_ = scrB[:, 0:n]
            vop((lambda e, o=b_, i=a, n=n: e.tensor_sub(o, i, ps2[:, 0:n])),
                [b_], [a, ps2[:, 0:n]], tag="ln_a2")
            vop((lambda e, o=dc, i=b_: e.tensor_add(o, o, i)),
                [dc], [dc, b_], tag="ln_a3")

    for tt in range(4):
        t0 = tt * 512
        ln_stats(lambda c, t0=t0: xfc(c, t0, 512), F32, t0, 512)
        ln_to_bf(lambda c, t0=t0: xfc(c, t0, 512),
                 lambda c, t0=t0: xnbc(c, t0, 512), t0, 512, s1)
    tap("xn1", xnb[:, :])

    dma("gpsimd", aab[:, 0:512].rearrange("p (kc m) -> p kc m", kc=8),
        AQ.rearrange("(kc p) m -> p kc m", p=128), tag="w_aq")
    dma("gpsimd", aab[:, 512:1024].rearrange("p (kc m) -> p kc m", kc=8),
        AV.rearrange("(kc p) m -> p kc m", p=128), tag="w_av")
    dma("gpsimd", aab[:, 1024:1536].rearrange("p (kc m) -> p kc m", kc=8),
        AO.rearrange("(kc p) m -> p kc m", p=128), tag="w_ao")
    dma("gpsimd", bbb2[0:64, 0:1024], BQ, tag="w_bq")
    dma("gpsimd", bbb2[64:128, 0:1024], BV, tag="w_bv")
    dma("gpsimd", bbb2[0:64, 1024:2048], BO, tag="w_bo")

    LQLV = 9216
    for tt in range(4):
        t0 = tt * 512
        rd_a = [aab[:, :], xnb[:, :]]
        for c in range(NCH):
            mm(ps0[0:64, 0:512], sap(aab, 0, 128, c * 64, 64),
               xnbc(c, t0, 512), c == 0, c == NCH - 1, tag="lq_mm", rd=rd_a)
        for c in range(NCH):
            mm(ps1[0:64, 0:512], sap(aab, 0, 128, 512 + c * 64, 64),
               xnbc(c, t0, 512), c == 0, c == NCH - 1, tag="lv_mm", rd=rd_a)
        vop((lambda e, o=sap(ar2, 0, 64, LQLV + t0, 512):
             e.tensor_copy(o, ps0[0:64, 0:512])),
            [sap(ar2, 0, 64, LQLV + t0, 512)], [ps0[0:64, 0:512]], tag="lq_ev")
        vop((lambda e, o=sap(ar2, 64, 64, LQLV + t0, 512):
             e.tensor_copy(o, ps1[0:64, 0:512])),
            [sap(ar2, 64, 64, LQLV + t0, 512)], [ps1[0:64, 0:512]], tag="lv_ev")

    R_AQ, R_DIFF, R_FK, R_VSM, R_WBF, R_WTS = 0, 2048, 4096, 4608, 5120, 5632
    R_WQ, R_WK, R_WV = 6144, 7168, 8192
    EBF2 = 11264
    R_FVB = 11776

    pb_aq = em.reg_sb(A("pb_aq", [128, 2048], BF16, offset=SB_BASE + 188416))
    pb_fk = em.reg_sb(A("pb_fk", [128, 512], BF16, offset=SB_BASE + 192512))
    pb_vsm = em.reg_sb(A("pb_vsm", [128, 512], BF16, offset=SB_BASE + 193536))
    pb_csp = em.reg_sb(A("pb_csp", [128, 32], F32, offset=SB_BASE + 194560))
    pb_diff = em.reg_sb(A("pb_diff", [128, 2048], BF16,
                          offset=SB_BASE + 196608))
    s1b = em.reg_sb(A("s1b", [128, 512], BF16, offset=SB_BASE + 157696))
    s2b = em.reg_sb(A("s2b", [128, 512], BF16, offset=SB_BASE + 159744))

    PAR = [dict(aq=(ar2, R_AQ), diff=(ar2, R_DIFF), fk=(ar2, R_FK),
                vsm=(ar2, R_VSM), csp=csp),
           dict(aq=(pb_aq, 0), diff=(pb_diff, 0), fk=(pb_fk, 0),
                vsm=(pb_vsm, 0), csp=pb_csp)]

    def wcol_dma(dst_c0, Wsrc, c, tag):
        dma("gpsimd",
            sap(ar2, 0, 128, dst_c0, 1024).rearrange("p (kc m) -> p kc m", kc=8),
            Wsrc[:, c * 128:(c + 1) * 128].rearrange("(kc p) m -> p kc m", p=128),
            tag=tag)

    def pool_cums(src, fk_dst, pk_dst, tagp, peng="vector", sc0=0):
        sA = [sap(src, 0, 128, j, 128, 4) for j in range(4)]
        s3s = s3[:, sc0:sc0 + 128]
        s4s = s4[:, sc0:sc0 + 128]
        vop((lambda e, a=sA[0], b=sA[1], o=s3s: e.tensor_add(o, a, b)),
            [s3s], [src[:, :]], eng=peng, tag=tagp + "1")
        vop((lambda e, a=sA[2], b=sA[3], o=s4s: e.tensor_add(o, a, b)),
            [s4s], [src[:, :]], eng=peng, tag=tagp + "2")
        vop((lambda e, o=fk_dst, a=s3s, b=s4s: e.tensor_add(o, a, b)),
            [fk_dst], [s3s, s4s], eng=peng, tag=tagp + "3")
        pA = [sap(pk_dst, 0, 128, j, 128, 4) for j in range(4)]
        vop((lambda e, o=pA[0], a=sA[0]: e.tensor_copy(o, a)),
            [pk_dst[:, :]], [src[:, :]], eng=peng, tag=tagp + "4")
        vop((lambda e, o=pA[1], a=sA[0], b=sA[1]: e.tensor_add(o, a, b)),
            [pk_dst[:, :]], [src[:, :]], eng=peng, tag=tagp + "5")
        vop((lambda e, o=pA[2], a=pA[1], b=sA[2]: e.tensor_add(o, a, b)),
            [pk_dst[:, :]], [pk_dst[:, :], src[:, :]], eng=peng, tag=tagp + "6")
        vop((lambda e, o=pA[3], a=fk_dst: e.tensor_copy(o, a)),
            [pk_dst[:, :]], [fk_dst], eng=peng, tag=tagp + "7")

    def prep_tt(c, tt):
        P = PAR[c % 2]
        aq_t, aq_b = P["aq"]
        df_t, df_b = P["diff"]
        fk_t, fk_b = P["fk"]
        vs_t, vs_b = P["vsm"]
        cspP = P["csp"]
        t0 = tt * 512
        sl0 = tt * 128
        if tt == 0:
            wcol_dma(R_WQ, WQ, c, "w_wq")
            wcol_dma(R_WK, WK, c, "w_wk")
            wcol_dma(R_WV, WV, c, "w_wv")
        rd_q = [sap(ar2, 0, 128, R_WQ, 1024), xnb[:, :]]
        rd_k = [sap(ar2, 0, 128, R_WK, 1024), xnb[:, :]]
        rd_v = [sap(ar2, 0, 128, R_WV, 1024), xnb[:, :]]
        rd_l = [sap(bbb2, 0, 128, 0, 2048), sap(ar2, 0, 128, LQLV, 2048)]
        for kc in range(NCH):
            mm(ps0[:, 0:512], sap(ar2, 0, 128, R_WQ + kc * 128, 128),
               xnbc(kc, t0, 512), kc == 0, False, tag="q_mm", rd=rd_q)
        mm(ps0[:, 0:512], sap(bbb2, 0, 64, c * 128, 128),
           sap(ar2, 0, 64, LQLV + t0, 512), False, True, tag="q_lora", rd=rd_l)
        for kc in range(NCH):
            mm(ps1[:, 0:512], sap(ar2, 0, 128, R_WK + kc * 128, 128),
               xnbc(kc, t0, 512), kc == 0, kc == NCH - 1, tag="k_mm", rd=rd_k)
        for kc in range(NCH):
            mm(ps2[:, 0:512], sap(ar2, 0, 128, R_WV + kc * 128, 128),
               xnbc(kc, t0, 512), kc == 0, False, tag="v_mm", rd=rd_v)
        mm(ps2[:, 0:512], sap(bbb2, 64, 64, c * 128, 128),
           sap(ar2, 64, 64, LQLV + t0, 512), False, True, tag="v_lora", rd=rd_l)

        vop((lambda e: e.tensor_copy(qb[:, :], ps0[:, 0:512])),
            [qb[:, :]], [ps0[:, 0:512]], tag="q_ev")
        mm(ps0[:, 0:512], pblk[:, :], qb[:, :], True, True, tag="q_rot")
        vop((lambda e: e.tensor_copy(rb[:, :], ps0[:, 0:512])),
            [rb[:, :]], [ps0[:, 0:512]], tag="q_rev")
        vop((lambda e, t0=t0: e.tensor_mul(s1b[:, :], qb[:, :],
                                           cph[:, t0:t0 + 512])),
            [s1b[:, :]], [qb[:, :], cph[:, t0:t0 + 512]], tag="q_c")
        vop((lambda e, t0=t0: e.tensor_mul(s2b[:, :], rb[:, :],
                                           sph[:, t0:t0 + 512])),
            [s2b[:, :]], [rb[:, :], sph[:, t0:t0 + 512]], tag="q_s")
        aq_sl = sap(aq_t, 0, 128, aq_b + t0, 512)
        vop((lambda e, o=aq_sl: e.tensor_add(o, s1b[:, :], s2b[:, :])),
            [aq_sl], [s1b[:, :], s2b[:, :]], tag="q_add")

        vop((lambda e: e.tensor_copy(kb[:, :], ps1[:, 0:512])),
            [kb[:, :]], [ps1[:, 0:512]], tag="k_ev")
        mm(ps1[:, 0:512], pblk[:, :], kb[:, :], True, True, tag="k_rot")
        vop((lambda e: e.tensor_copy(rb[:, :], ps1[:, 0:512])),
            [rb[:, :]], [ps1[:, 0:512]], tag="k_rev")
        vop((lambda e, t0=t0: e.tensor_mul(s1b[:, :], kb[:, :],
                                           cph[:, t0:t0 + 512])),
            [s1b[:, :]], [kb[:, :], cph[:, t0:t0 + 512]], tag="k_c")
        vop((lambda e, t0=t0: e.tensor_mul(s2b[:, :], rb[:, :],
                                           sph[:, t0:t0 + 512])),
            [s2b[:, :]], [rb[:, :], sph[:, t0:t0 + 512]], tag="k_s")
        vop((lambda e: e.tensor_add(kshb[:, :], s1b[:, :], s2b[:, :])),
            [kshb[:, :]], [s1b[:, :], s2b[:, :]], tag="k_add")

        vop((lambda e: e.tensor_copy(vb[:, :], ps2[:, 0:512])),
            [vb[:, :]], [ps2[:, 0:512]], tag="v_ev")
        v3 = bass.AP(vb, 0, [[512, 128], [4, 128], [1, 4]])
        p3 = bass.AP(phb, 0, [[4, 128], [0, 128], [1, 4]])
        vs3 = bass.AP(vshb, 0, [[512, 128], [4, 128], [1, 4]])
        vop((lambda e, o=vs3, a=v3, b=p3: e.tensor_tensor(o, a, b, ALU.mult)),
            [vshb[:, :]], [vb[:, :], phb[:, :]], eng="gpsimd", tag="v_ph")

        fk_sl = sap(fk_t, 0, 128, fk_b + sl0, 128)
        pool_cums(kshb, fk_sl, pkb, "kp")
        if c == 0 and tt == 0:
            tap("kshb0", kshb[:, :])
            tap("pkb0", pkb[:, :])
        vop((lambda e, o=prb[:, :], a=aq_sl, b=pkb[:, :]:
             e.tensor_mul(o, a, b)),
            [prb[:, :]], [aq_sl, pkb[:, :]], tag="prod")
        if c == 0 and tt == 0:
            tap("prb0", prb[:, :])
        for st in range(4):
            ti = tt * 4 + st
            tp(ps7[:, 384:512], prb[:, st * 128:(st + 1) * 128], idb[:, :],
               tag="cs_tp")
            vop((lambda e, o=cspP[:, 2 * ti:2 * ti + 1]:
                 e.reduce_sum(o, ps7[:, 384:448], axis=AX.X)),
                [cspP[:, 2 * ti:2 * ti + 1]], [ps7[:, 384:448]], tag="cs_r0")
            vop((lambda e, o=cspP[:, 2 * ti + 1:2 * ti + 2]:
                 e.reduce_sum(o, ps7[:, 448:512], axis=AX.X)),
                [cspP[:, 2 * ti + 1:2 * ti + 2]], [ps7[:, 448:512]],
                tag="cs_r1")

        fvb = sap(ar2, 0, 128, R_FVB, 128)
        pool_cums(vshb, fvb, pvb, "vp", peng="gpsimd", sc0=128)
        dsl = [sap(df_t, 0, 128, df_b + t0 + j, 128, 4) for j in range(4)]
        dregion = sap(df_t, 0, 128, df_b + t0, 512)
        for j in range(3):
            vop((lambda e, o=dsl[j], a=sap(pvb, 0, 128, j, 128, 4), f=fvb:
                 e.tensor_sub(o, a, f)),
                [dregion], [pvb[:, :], fvb], eng="gpsimd", tag="diff")
        vop((lambda e, o=dsl[3]: e.memset(o, 0)),
            [dregion], [], eng="gpsimd", tag="diff0")
        tp(ps7[:, 256:384], fvb, idb[:, :], tag="vsm_tp")
        vop((lambda e, o=sap(vs_t, 0, 128, vs_b + sl0, 128):
             e.tensor_copy(o, ps7[:, 256:384])),
            [sap(vs_t, 0, 128, vs_b + sl0, 128)], [ps7[:, 256:384]],
            tag="vsm_ev")

    def stageA(c, i):
        P = PAR[c % 2]
        aq_t, aq_b = P["aq"]
        fk_t, fk_b = P["fk"]
        cspP = P["csp"]
        ti, h = divmod(i, 2)
        c0 = 32 * ti
        ncols = c0 + 32
        nsc = (ncols + 127) // 128
        sb = ps4 if i % 2 == 0 else ps3
        ebase = R_WBF if i % 2 == 0 else EBF2
        lhsT = sap(aq_t, h * 64, 64, aq_b + ti * 128, 128)
        rhs = sap(fk_t, h * 64, 64, fk_b, ncols)
        mm(sb[:, 0:ncols], lhsT, rhs, True, True, tag="scores",
           rd=[sap(aq_t, 0, 128, aq_b, 2048), sap(fk_t, 0, 128, fk_b, 512)])
        cscol = cspP[:, 2 * ti + h:2 * ti + h + 1]
        csb = bass.AP(cspP, 2 * ti + h, [[32, 128], [0, 32]])
        bnd = sb[:, c0:c0 + 32]
        vop((lambda e, b=bnd, m=csb: e.copy_predicated(b, selu8[:, :], m)),
            [bnd], [bnd, selu8[:, :], cscol], tag="band1")
        vop((lambda e, b=bnd: e.tensor_add(b, b, bneg[:, :])),
            [bnd], [bnd, bneg[:, :]], tag="band4")
        eccol = rows6[:, 4 + i % 2:5 + i % 2]
        vop((lambda e, o=eccol, sc=cscol: e.activation(
            o, sc, AF.Exp, scale=INV16)),
            [eccol], [cscol], eng="scalar", tag="ecx")
        ebf = sap(ar2, 0, 128, ebase, ncols)
        srow = rows6[:, i % 2:i % 2 + 1]
        vop((lambda e, o=ebf, n=ncols, s=sb, sr=srow: e.activation(
            o, s[:, 0:n], AF.Exp, scale=INV16, accum_out=sr)),
            [ebf, srow], [sb[:, 0:ncols]], eng="scalar", tag="sexp")
        if ncols % 128:
            pad = sap(ar2, 0, 128, ebase + ncols, nsc * 128 - ncols)
            vop((lambda e, o=pad: e.memset(o, 0)), [pad], [], tag="wpad")
        inv = rows6[:, 2 + i % 2:3 + i % 2]
        vop((lambda e, o=inv, sr=srow: e.reciprocal(o, sr)),
            [inv], [srow], tag="sinv")

    def stageB(c, i):
        P = PAR[c % 2]
        df_t, df_b = P["diff"]
        vs_t, vs_b = P["vsm"]
        ti, h = divmod(i, 2)
        c0 = 32 * ti
        ncols = c0 + 32
        nsc = (ncols + 127) // 128
        ebase = R_WBF if i % 2 == 0 else EBF2
        ob = ps6
        inv = rows6[:, 2 + i % 2:3 + i % 2]
        if h == 0:
            tp(ps7[:, 0:128], sap(df_t, 0, 128, df_b + ti * 128, 128),
               idb[:, :], tag="dtm_tp")
        for j in range(nsc):
            tp(ps5[:, j * 128:(j + 1) * 128],
               sap(ar2, 0, 128, ebase + j * 128, 128), idb[:, :], tag="w_tp")
        wts = sap(ar2, 0, 128, R_WTS, nsc * 128)
        vop((lambda e, o=wts, n=nsc * 128: e.copy(o, ps5[:, 0:n])),
            [wts], [ps5[:, 0:nsc * 128]], eng="scalar", tag="w_ev")
        rd_vm = [sap(ar2, 0, 128, R_WTS, 512), sap(vs_t, 0, 128, vs_b, 512)]
        for j in range(nsc):
            mm(ob[:, 0:64], sap(ar2, 0, 128, R_WTS + j * 128, 128),
               sap(vs_t, 0, 128, vs_b + j * 128 + h * 64, 64),
               j == 0, j == nsc - 1, tag="vmm", rd=rd_vm)
        eccol = rows6[:, 4 + i % 2:5 + i % 2]
        sA = s64a[:, :] if i % 2 == 0 else ef[:, 0:64]
        sB = s64b[:, :] if i % 2 == 0 else ef[:, 64:128]
        vop((lambda e, h=h, o=sA, ec=eccol: e.activation(
            o, ps7[:, h * 64:h * 64 + 64], AF.Copy, scale=ec)),
            [sA], [ps7[:, 0:128], eccol], eng="scalar", tag="corr1")
        vop((lambda e, o=ob, a=sA, b=sB: e.tensor_add(b, a, o[:, 0:64])),
            [sB], [sA, ob[:, 0:64]], tag="corr2")
        vop((lambda e, h=h, iv=inv, b=sB: e.scalar_tensor_tensor(
            otm[:, h * 64:(h + 1) * 64], b, iv, ph2t[:, :],
            ALU.mult, ALU.mult)),
            [otm[:, h * 64:(h + 1) * 64]], [sB, inv, ph2t[:, :]],
            tag="ophase")
        if h == 1:
            tp(ps7[:, 128:256], otm[:, 0:128], idb[:, :], tag="o_tp")
            vop((lambda e, o=ar1c(c, ti * 128, 128): e.copy(
                o, ps7[:, 128:256])),
                [ar1c(c, ti * 128, 128)], [ps7[:, 128:256]], eng="scalar",
                tag="o_ev")

    for c in range(NCH + 1):
        for tt in range(4):
            if c < NCH:
                prep_tt(c, tt)
            if c > 0:
                if tt == 0:
                    stageA(c - 1, 0)
                    stageA(c - 1, 1)
                for i in range(tt * 8, tt * 8 + 8):
                    stageB(c - 1, i)
                    if i + 2 < 32:
                        stageA(c - 1, i + 2)
        if c == 1:
            tap("aq0", sap(ar2, 0, 128, R_AQ, S))
            tap("fullk0", sap(ar2, 0, 128, R_FK, NSLOT))
            tap("vsm0", sap(ar2, 0, 128, R_VSM, NSLOT))
            tap("diff0", sap(ar2, 0, 128, R_DIFF, S))
            tap("csp0", csp[:, :])

    tap("outfm", ar1[:, :])

    for tt in range(4):
        t0 = tt * 512
        for c in range(NCH):
            mm(ps0[0:64, 0:512], sap(aab, 0, 128, 1024 + c * 64, 64),
               ar1c(c, t0, 512), c == 0, c == NCH - 1, tag="lo_mm",
               rd=[aab[:, :], ar1[:, :]])
        vop((lambda e, o=lob[:, t0:t0 + 512]: e.tensor_copy(
            o, ps0[0:64, 0:512])),
            [lob[:, t0:t0 + 512]], [ps0[0:64, 0:512]], tag="lo_ev")
    for c in range(NCH):
        wcol_dma(R_WQ, WO, c, "w_wo")
        for tt in range(4):
            t0 = tt * 512
            rd_y = [sap(ar2, 0, 128, R_WQ, 1024), ar1[:, :]]
            rd_yl = [sap(bbb2, 0, 128, 0, 2048), lob[:, :]]
            for kc in range(NCH):
                mm(ps1[:, 0:512], sap(ar2, 0, 128, R_WQ + kc * 128, 128),
                   ar1c(kc, t0, 512), kc == 0, False, tag="y_mm", rd=rd_y)
            mm(ps1[:, 0:512], sap(bbb2, 0, 64, 1024 + c * 128, 128),
               lob[:, t0:t0 + 512], False, True, tag="y_lora", rd=rd_yl)
            vop((lambda e, o=xnbc(c, t0, 512): e.tensor_copy(
                o, ps1[:, 0:512])),
                [xnbc(c, t0, 512)], [ps1[:, 0:512]], tag="y_ev")
    tap("ybf", xnb[:, :])

    for tt in range(4):
        t0 = tt * 512
        ln_stats(lambda c, t0=t0: xnbc(c, t0, 512), BF16, t0, 512)
        ln_res_add(lambda c, t0=t0: xnbc(c, t0, 512), BF16,
                   lambda c, t0=t0: xfc(c, t0, 512), t0, 512, s1, s2, s4)
    tap("x1", xf[:, :])

    for tt in range(4):
        t0 = tt * 512
        ln_stats(lambda c, t0=t0: xfc(c, t0, 512), F32, t0, 512)
        ln_to_bf(lambda c, t0=t0: xfc(c, t0, 512),
                 lambda c, t0=t0: xnbc(c, t0, 512), t0, 512, s1)

    FB = 8

    def wslice(buf, base, c0, n):
        return sap(buf, 0, 128, base + c0, n)

    for th in range(2):
        th0 = th * 1024
        pending_down = []

        def emit_down(wb_, base_d, hb, tt, e_i, fb):
            rd_d = [wslice(wb_, base_d, 0, 4096), sap(ar1, 0, 128, hb, 2048)]
            for hc in range(NCH):
                dps = ps2 if hc % 2 == 0 else ps3
                for fc in range(4):
                    mm(dps[:, 0:512],
                       wslice(wb_, base_d, fc * 1024 + hc * 128, 128),
                       sap(ar1, 0, 128, hb + fc * 512, 512),
                       fc == 0, fc == 3, tag="d_mm", rd=rd_d)
                acc_sl = accf[:, hc * 1024 + tt * 512:
                              hc * 1024 + tt * 512 + 512]
                if e_i == 0 and fb == 0:
                    vop((lambda e, o=acc_sl, i=dps[:, 0:512]:
                         e.tensor_copy(o, i)),
                        [acc_sl], [dps[:, 0:512]], tag="acc_cp")
                else:
                    vop((lambda e, o=acc_sl, i=dps[:, 0:512]:
                         e.tensor_add(o, o, i)),
                        [acc_sl], [acc_sl, dps[:, 0:512]], tag="acc_add")

        for e_i in range(NE):
            for fb in range(FB):
                par = (e_i * FB + fb) % 2
                wb_ = ar1 if par == 0 else ar2
                base_g, base_u, base_d = 0, 4096, 8192
                dma("gpsimd",
                    sap(wb_, 0, 128, base_g, 4096).rearrange(
                        "p (kc f) -> p kc f", kc=8),
                    WG[e_i, :, fb * 512:(fb + 1) * 512].rearrange(
                        "(kc p) f -> p kc f", p=128), tag="w_g")
                dma("gpsimd",
                    sap(wb_, 0, 128, base_u, 4096).rearrange(
                        "p (kc f) -> p kc f", kc=8),
                    WU[e_i, :, fb * 512:(fb + 1) * 512].rearrange(
                        "(kc p) f -> p kc f", p=128), tag="w_u")
                dma("gpsimd",
                    sap(wb_, 0, 128, base_d, 4096).rearrange(
                        "p (fc m) -> p fc m", fc=4),
                    WD_[e_i, fb * 512:(fb + 1) * 512, :].rearrange(
                        "(fc p) m -> p fc m", p=128), tag="w_d")
                for tt in range(2):
                    t0 = th0 + tt * 512
                    hb = 12288 + tt * 2048
                    for ft in range(4):
                        gps = ps0 if ft % 2 == 0 else ps4
                        ups = ps1 if ft % 2 == 0 else ps6
                        rd_g = [wslice(wb_, base_g, 0, 4096), xnb[:, :]]
                        rd_u = [wslice(wb_, base_u, 0, 4096), xnb[:, :]]
                        for kc in range(NCH):
                            mm(gps[:, 0:512],
                               wslice(wb_, base_g, kc * 512 + ft * 128, 128),
                               xnb[:, kc * S + t0:kc * S + t0 + 512],
                               kc == 0, kc == NCH - 1, tag="g_mm", rd=rd_g)
                        for kc in range(NCH):
                            mm(ups[:, 0:512],
                               wslice(wb_, base_u, kc * 512 + ft * 128, 128),
                               xnb[:, kc * S + t0:kc * S + t0 + 512],
                               kc == 0, kc == NCH - 1, tag="u_mm", rd=rd_u)
                        sg = fs[:, 0:512] if ft % 2 == 0 else fs[:, 512:1024]
                        vop((lambda e, o=sg, i=gps[:, 0:512]: e.activation(
                            o, i, AF.Silu)),
                            [sg], [gps[:, 0:512]], eng="scalar", tag="silu")
                        hsl = sap(ar1, 0, 128, hb + ft * 512, 512)
                        vop((lambda e, o=hsl, a=sg, b=ups[:, 0:512]:
                             e.tensor_mul(o, a, b)),
                            [hsl], [sg, ups[:, 0:512]], tag="hmul")
                        if tt == 1 and ft == 0:
                            emit_down(wb_, base_d, 12288, 0, e_i, fb)
                    if tt == 1:
                        emit_down(wb_, base_d, hb, 1, e_i, fb)
        if th == 0:
            tap("acc", accf[:, :])
        for tt in range(2):
            t0 = tt * 512
            ln_stats(lambda c, t0=t0: accf[:, c * 1024 + t0:c * 1024 + t0 + 512],
                     F32, t0, 512, sqscr=(fs[:, 0:512], fs[:, 512:1024]))
            ln_res_add(lambda c, t0=t0: accf[:, c * 1024 + t0:
                                             c * 1024 + t0 + 512],
                       F32,
                       lambda c, t0=t0, th0=th0: xfc(c, th0 + t0, 512),
                       t0, 512, fs[:, 1024:1536], fs[:, 1536:2048], None)

    for ti in range(16):
        stg = stage if ti % 2 == 0 else stage2
        for c in range(NCH):
            pst = ps4 if c % 2 == 0 else ps6
            tp(pst[:, 0:128], xfc(c, ti * 128, 128), idf[:, :], tag="ot_tp")
            vop((lambda e, o=stg[:, c * 128:(c + 1) * 128], i=pst[:, 0:128]:
                 e.tensor_copy(o, i)),
                [stg[:, c * 128:(c + 1) * 128]], [pst[:, 0:128]], tag="ot_ev")
        dma("sync", Y[ti * 128:(ti + 1) * 128, :], stg[:, :], tag="out")

    from contextlib import ExitStack
    with ExitStack() as ctx:
        for k in SEM_KEYS:
            sems[k] = ctx.enter_context(nc.semaphore(k))
        block = ctx.enter_context(nc.Block())
        em.build(block, sems, strict=strict)
    return nc, em


def kernel(L0_attn_hidden_states, L0_attn_Wq, L0_attn_Wk, L0_attn_Wv,
           L0_attn_Wo, L0_attn_Aq, L0_attn_Bq, L0_attn_Av, L0_attn_Bv,
           L0_attn_Ao, L0_attn_Bo, L0_ffn_Wg, L0_ffn_Wu, L0_ffn_Wd):
    x = np.ascontiguousarray(np.asarray(L0_attn_hidden_states, dtype=np.float32))
    consts = make_consts()
    shared = dict(
        wq=np.ascontiguousarray(np.asarray(L0_attn_Wq, np.float32)),
        wk=np.ascontiguousarray(np.asarray(L0_attn_Wk, np.float32)),
        wv=np.ascontiguousarray(np.asarray(L0_attn_Wv, np.float32)),
        wo=np.ascontiguousarray(np.asarray(L0_attn_Wo, np.float32)),
        aq=np.ascontiguousarray(np.asarray(L0_attn_Aq, np.float32)),
        bq=np.ascontiguousarray(np.asarray(L0_attn_Bq, np.float32)),
        av=np.ascontiguousarray(np.asarray(L0_attn_Av, np.float32)),
        bv=np.ascontiguousarray(np.asarray(L0_attn_Bv, np.float32)),
        ao=np.ascontiguousarray(np.asarray(L0_attn_Ao, np.float32)),
        bo=np.ascontiguousarray(np.asarray(L0_attn_Bo, np.float32)),
        wg=np.ascontiguousarray(np.asarray(L0_ffn_Wg, np.float32)),
        wu=np.ascontiguousarray(np.asarray(L0_ffn_Wu, np.float32)),
        wd=np.ascontiguousarray(np.asarray(L0_ffn_Wd, np.float32)),
        **consts,
    )
    nc, _ = build_program(strict=("vector", "scalar"))
    in_maps = [dict(x=np.ascontiguousarray(x[b]), **shared) for b in range(8)]
    res = run_bass_kernel_spmd(nc, in_maps, core_ids=list(range(8)))
    out = np.stack([r["y"] for r in res.results], axis=0).astype(np.float32)
    return out


# version 10
# speedup vs baseline: 1.1229x; 1.1229x over previous
import numpy as np
import ml_dtypes
import concourse.bass as bass
import concourse.mybir as mybir
from concourse.bass_utils import run_bass_kernel_spmd

F32 = mybir.dt.float32
BF16 = mybir.dt.bfloat16
AF = mybir.ActivationFunctionType
ALU = mybir.AluOpType
AX = mybir.AxisListType

S = 2048
D = 1024
NCH = 8
NH = 16
HD = 64
KF = 4
NSLOT = 512
LR = 64
NE = 8
FF = 4096
EPS = 1e-5
INV16 = 1.0 / 16.0

SB_BASE = 17472

ENG_SEM = {"sync": "q_sy", "gpsimd": "q_gp", "vector": "e_v", "scalar": "e_s",
           "tensor": "e_t"}
COMPUTE_SEM = {"sync": "e_sy", "gpsimd": "e_p", "vector": "e_v", "scalar": "e_s",
               "tensor": "e_t"}
DMA_POOL = 8
SEM_KEYS = (["e_v", "e_s", "e_t", "e_p", "e_sy"]
            + [f"q_sy{i}" for i in range(DMA_POOL)]
            + [f"q_gp{i}" for i in range(DMA_POOL)])


class Em:
    def __init__(self, nc):
        self.nc = nc
        self.ops = []
        self.sb_base = {}
        self.psum_id = {}
        self.writes = []
        self.reads = []

    def reg_sb(self, t):
        self.sb_base[t.name] = t.manual_sbuf_range[0]
        return t

    def reg_ps(self, t):
        self.psum_id[t.name] = len(self.psum_id)
        return t

    def _box(self, ap):
        t = ap.tensor
        tn = t.name
        if tn in self.psum_id:
            i = self.psum_id[tn]
            return (0, 128, 1 << 30 | i << 16, 1 << 30 | (i << 16) + 1)
        if tn not in self.sb_base:
            return None
        base = self.sb_base[tn]
        esz = 2 if ap.dtype == BF16 else 4
        shp = t.shape
        fbytes = 1
        for s in shp[1:]:
            fbytes *= s
        fbytes *= (2 if t.dtype == BF16 else 4)
        p0 = ap.base_partition()
        pn = ap.partition_size()
        off_b = int(ap.offset) * esz
        c0 = off_b - p0 * fbytes
        ext = 1
        pairs = [list(p) for p in ap.ap]
        for st, n in pairs[1:]:
            ext += (n - 1) * abs(st)
        ext *= esz
        return (p0, p0 + pn, base + c0, base + c0 + ext)

    @staticmethod
    def _ov(a, b):
        return a[0] < b[1] and b[0] < a[1] and a[2] < b[3] and b[2] < a[3]

    def op(self, eng, fn, reads=(), writes=(), dma=False, tag=""):
        idx = len(self.ops)
        rb = [b for b in (self._box(a) for a in reads) if b is not None]
        wb = [b for b in (self._box(a) for a in writes) if b is not None]
        deps = set()
        for b in rb:
            for e in self.writes:
                if self._ov(b, e[0]):
                    deps.add(e[1])
        for b in wb:
            for e in self.writes:
                if self._ov(b, e[0]):
                    deps.add(e[1])
            for e in self.reads:
                if self._ov(b, e[0]):
                    deps.add(e[1])
        qidx = None
        if dma:
            if not hasattr(self, "qcount"):
                self.qcount = {}
            qidx = self.qcount.get(eng, 0)
            self.qcount[eng] = qidx + 1
        self.ops.append(dict(eng=eng, fn=fn, dma=dma, deps=deps, tag=tag,
                             idx=idx, need_inc=False, qidx=qidx))
        for b in rb:
            for e in self.reads:
                if e[0] == b and self.ops[e[1]]["eng"] == eng and \
                   self.ops[e[1]]["dma"] == dma:
                    e[1] = idx
                    break
            else:
                self.reads.append([b, idx])
        for b in wb:
            for e in self.writes:
                if e[0] == b:
                    e[1] = idx
                    break
            else:
                self.writes.append([b, idx])
        return idx

    def build(self, block, sems, strict=()):
        strict = set(strict)
        ops = self.ops
        n = len(ops)
        for o in ops:
            drop = []
            for d in o["deps"]:
                dop = ops[d]
                if dop["eng"] == o["eng"] and not dop["dma"] and not o["dma"] \
                        and o["eng"] not in strict:
                    drop.append(d)
            for d in drop:
                o["deps"].discard(d)
            for d in o["deps"]:
                ops[d]["need_inc"] = True
        for o in ops:
            if o["dma"]:
                o["need_inc"] = True
        counters = {}
        prev_pool_op = {}
        for o in ops:
            if o["dma"]:
                key = ENG_SEM[o["eng"]] + str(o["qidx"] % DMA_POOL)
                rnd = o["qidx"] // DMA_POOL
                o["inc_val"] = 16 * (rnd + 1)
                counters[key] = o["inc_val"]
                o["self_wait"] = (key, 16 * rnd, prev_pool_op.get(key)) \
                    if rnd > 0 else None
                prev_pool_op[key] = o["idx"]
            else:
                key = COMPUTE_SEM[o["eng"]]
                if o["need_inc"]:
                    counters[key] = counters.get(key, 0) + 1
                    o["inc_val"] = counters[key]
                else:
                    o["inc_val"] = None
                o["self_wait"] = None
            o["sem_key"] = key
        self.totals = dict(counters)
        eng_clock = {e: {} for e in ENG_SEM}
        op_vc = [None] * n
        for o in ops:
            e = o["eng"]
            clk = eng_clock[e]
            waits = []
            if o["self_wait"] is not None:
                sk, tv, pidx = o["self_wait"]
                if clk.get(sk, 0) < tv:
                    waits.append((sk, tv))
                    if pidx is not None:
                        for k, v in op_vc[pidx].items():
                            if clk.get(k, 0) < v:
                                clk[k] = v
                    if clk.get(sk, 0) < tv:
                        clk[sk] = tv
            for d in sorted(o["deps"]):
                dop = ops[d]
                sk = dop["sem_key"]
                tv = dop["inc_val"]
                assert tv is not None
                if clk.get(sk, 0) < tv:
                    waits.append((sk, tv))
                    for k, v in op_vc[d].items():
                        if clk.get(k, 0) < v:
                            clk[k] = v
                    if clk.get(sk, 0) < tv:
                        clk[sk] = tv
            wd = {}
            for sk, tv in waits:
                wd[sk] = max(wd.get(sk, 0), tv)
            o["waits"] = list(wd.items())
            vc = dict(clk)
            if o["inc_val"] is not None:
                vc[o["sem_key"]] = o["inc_val"]
            op_vc[o["idx"]] = vc
        for eng_name in ("sync", "gpsimd", "vector", "scalar", "tensor"):
            myops = [o for o in ops if o["eng"] == eng_name]
            if not myops and eng_name != "sync":
                continue

            def body(e, myops=myops, eng_name=eng_name):
                for o in myops:
                    for sk, tv in o["waits"]:
                        e.wait_ge(sems[sk], tv)
                    inst = o["fn"](e)
                    if o["inc_val"] is not None:
                        inst.then_inc(sems[o["sem_key"]],
                                      16 if o["dma"] else 1)
                if eng_name == "sync":
                    for qk, tv in sorted(self.totals.items()):
                        if qk.startswith("q_"):
                            e.wait_ge(sems[qk], tv)
            getattr(block, eng_name)(body)


def make_consts():
    H4 = np.array([[1, 1, 1, 1], [1, -1, 1, -1], [1, 1, -1, -1], [1, -1, -1, 1]],
                  np.float64)
    inv_freq = 1.0 / 10000.0 ** (np.arange(0, HD, 2, dtype=np.float64) / HD)
    pos = np.arange(S, dtype=np.float64)
    freqs = pos[:, None] * inv_freq[None, :]
    emb = np.concatenate([freqs, freqs], axis=-1)
    cos, sin = np.cos(emb), np.sin(emb)
    phase = H4[np.arange(S) % 4][:, np.arange(HD) % 4]
    cph64 = (cos * phase).T
    sph64 = (sin * phase).T
    cph = np.concatenate([cph64, cph64], axis=0)
    sph = np.concatenate([sph64, sph64], axis=0)
    p = np.arange(128)
    phb = H4[:, p % 4].T
    r = np.arange(128)
    dd = np.arange(HD)
    ph2t = H4[r % 4][:, dd % 4]
    j32 = np.arange(32)
    sel = (j32[None, :] == (r[:, None] // 4)).astype(np.float64)
    seli = 1.0 - sel
    bneg = np.where(j32[None, :] > (r[:, None] // 4), -1e30, 0.0)
    pblk = np.zeros((128, 128))
    for a in (0, 64):
        for dp in range(32):
            pblk[a + dp + 32, a + dp] = -1.0
        for dp in range(32, 64):
            pblk[a + dp - 32, a + dp] = 1.0
    b16 = ml_dtypes.bfloat16
    return {
        "c_cph": cph.astype(b16), "c_sph": sph.astype(b16),
        "c_phb": phb.astype(b16), "c_ph2t": ph2t.astype(np.float32),
        "c_sel": sel.astype(np.float32), "c_seli": seli.astype(np.float32),
        "c_bneg": bneg.astype(np.float32),
        "c_selb": sel.astype(b16), "c_selu8": sel.astype(np.uint8),
        "c_idb": np.eye(128, dtype=b16), "c_idf": np.eye(128, dtype=np.float32),
        "c_pblk": pblk.astype(b16),
    }


def build_program(taps=(), strict=()):
    nc = bass.Bass("TRN2", target_bir_lowering=False)
    taps = set(taps)

    X = nc.dram_tensor("x", [S, D], F32, kind="ExternalInput").ap()
    Y = nc.dram_tensor("y", [S, D], F32, kind="ExternalOutput").ap()
    WQ = nc.dram_tensor("wq", [D, D], F32, kind="ExternalInput").ap()
    WK = nc.dram_tensor("wk", [D, D], F32, kind="ExternalInput").ap()
    WV = nc.dram_tensor("wv", [D, D], F32, kind="ExternalInput").ap()
    WO = nc.dram_tensor("wo", [D, D], F32, kind="ExternalInput").ap()
    AQ = nc.dram_tensor("aq", [D, LR], F32, kind="ExternalInput").ap()
    BQ = nc.dram_tensor("bq", [LR, D], F32, kind="ExternalInput").ap()
    AV = nc.dram_tensor("av", [D, LR], F32, kind="ExternalInput").ap()
    BV = nc.dram_tensor("bv", [LR, D], F32, kind="ExternalInput").ap()
    AO = nc.dram_tensor("ao", [D, LR], F32, kind="ExternalInput").ap()
    BO = nc.dram_tensor("bo", [LR, D], F32, kind="ExternalInput").ap()
    WG = nc.dram_tensor("wg", [NE, D, FF], F32, kind="ExternalInput").ap()
    WU = nc.dram_tensor("wu", [NE, D, FF], F32, kind="ExternalInput").ap()
    WD_ = nc.dram_tensor("wd", [NE, FF, D], F32, kind="ExternalInput").ap()
    CONST_SPECS = {
        "c_cph": ([128, S], BF16), "c_sph": ([128, S], BF16),
        "c_phb": ([128, 4], BF16), "c_ph2t": ([128, HD], F32),
        "c_sel": ([128, 32], F32), "c_seli": ([128, 32], F32),
        "c_bneg": ([128, 32], F32), "c_selb": ([128, 32], BF16),
        "c_selu8": ([128, 32], mybir.dt.uint8),
        "c_idb": ([128, 128], BF16),
        "c_idf": ([128, 128], F32), "c_pblk": ([128, 128], BF16),
    }
    CD = {k: nc.dram_tensor(k, sh, dt, kind="ExternalInput").ap()
          for k, (sh, dt) in CONST_SPECS.items()}
    tap_d = {}
    for name, sh, dt in (
        ("xn1", [128, NCH * S], F32), ("outfm", [128, NCH * S], F32),
        ("ybf", [128, NCH * S], F32), ("x1", [128, NCH * S], F32),
        ("acc", [128, NCH * 1024], F32), ("aq0", [128, S], F32),
        ("fullk0", [128, NSLOT], F32), ("vsm0", [128, NSLOT], F32),
        ("diff0", [128, S], F32), ("csp0", [128, 32], F32),
        ("kshb0", [128, 512], F32), ("pkb0", [128, 512], F32),
        ("prb0", [128, 512], F32),
    ):
        if name in taps:
            tap_d[name] = nc.dram_tensor("tap_" + name, sh, dt,
                                         kind="ExternalOutput").ap()

    em = Em(nc)
    A = nc.alloc_sbuf_tensor_at

    xf = em.reg_sb(A("xf", [128, NCH * S], F32, offset=SB_BASE + 0))
    xnb = em.reg_sb(A("xnb", [128, NCH * S], BF16, offset=SB_BASE + 65536))
    ar1 = em.reg_sb(A("ar1", [128, NCH * S], BF16, offset=SB_BASE + 98304))
    ar2 = em.reg_sb(A("ar2", [128, 12288], BF16, offset=SB_BASE + 131072))
    accf = em.reg_sb(A("accf", [128, 8192], F32, offset=SB_BASE + 155648))
    ef = em.reg_sb(A("ef", [128, 512], F32, offset=SB_BASE + 155648))
    s1 = em.reg_sb(A("s1", [128, 512], F32, offset=SB_BASE + 157696))
    s2 = em.reg_sb(A("s2", [128, 512], F32, offset=SB_BASE + 159744))
    s3 = em.reg_sb(A("s3", [128, 512], F32, offset=SB_BASE + 161792))
    s4 = em.reg_sb(A("s4", [128, 512], F32, offset=SB_BASE + 163840))
    s32 = em.reg_sb(A("s32", [128, 32], F32, offset=SB_BASE + 165888))
    s64a = em.reg_sb(A("s64a", [128, 64], F32, offset=SB_BASE + 166016))
    s64b = em.reg_sb(A("s64b", [128, 64], F32, offset=SB_BASE + 166272))
    rows6 = em.reg_sb(A("rows6", [128, 8], F32, offset=SB_BASE + 166528))
    csp = em.reg_sb(A("csp", [128, 32], F32, offset=SB_BASE + 166560))
    qb = em.reg_sb(A("qb", [128, 512], BF16, offset=SB_BASE + 166912))
    rb = em.reg_sb(A("rb", [128, 512], BF16, offset=SB_BASE + 167936))
    kb = em.reg_sb(A("kb", [128, 512], BF16, offset=SB_BASE + 168960))
    kshb = em.reg_sb(A("kshb", [128, 512], BF16, offset=SB_BASE + 169984))
    vb = em.reg_sb(A("vb", [128, 512], BF16, offset=SB_BASE + 171008))
    vshb = em.reg_sb(A("vshb", [128, 512], BF16, offset=SB_BASE + 172032))
    pkb = em.reg_sb(A("pkb", [128, 512], BF16, offset=SB_BASE + 173056))
    pvb = em.reg_sb(A("pvb", [128, 512], BF16, offset=SB_BASE + 174080))
    prb = em.reg_sb(A("prb", [128, 512], BF16, offset=SB_BASE + 175104))
    otm = em.reg_sb(A("otm", [128, 128], BF16, offset=SB_BASE + 176128))
    aab = em.reg_sb(A("aab", [128, 1536], BF16, offset=SB_BASE + 176384))
    bbb2 = em.reg_sb(A("bbb2", [128, 2048], BF16, offset=SB_BASE + 179456))
    lob = em.reg_sb(A("lob", [64, 2048], BF16, offset=SB_BASE + 183552))
    stage = em.reg_sb(A("stage", [128, 1024], F32, offset=SB_BASE + 155648))
    stage2 = em.reg_sb(A("stage2", [128, 1024], F32, offset=SB_BASE + 159744))
    fs = em.reg_sb(A("fs", [128, 2048], F32, offset=SB_BASE + 188416))
    rowsA = em.reg_sb(A("rowsA", [1, 1024], F32, offset=SB_BASE + 196608))
    cph = em.reg_sb(A("cph", [128, S], BF16, offset=SB_BASE + 200704))
    sph = em.reg_sb(A("sph", [128, S], BF16, offset=SB_BASE + 204800))
    ph2t = em.reg_sb(A("ph2t", [128, HD], F32, offset=SB_BASE + 208896))
    sel = em.reg_sb(A("sel", [128, 32], F32, offset=SB_BASE + 209152))
    seli = em.reg_sb(A("seli", [128, 32], F32, offset=SB_BASE + 209280))
    bneg = em.reg_sb(A("bneg", [128, 32], F32, offset=SB_BASE + 209408))
    idb = em.reg_sb(A("idb", [128, 128], BF16, offset=SB_BASE + 209536))
    idf = em.reg_sb(A("idf", [128, 128], F32, offset=SB_BASE + 209792))
    pblk = em.reg_sb(A("pblk", [128, 128], BF16, offset=SB_BASE + 210304))
    phb = em.reg_sb(A("phb", [128, 4], BF16, offset=SB_BASE + 210560))
    epsc = em.reg_sb(A("epsc", [128, 1], F32, offset=SB_BASE + 210592))
    ones1f = em.reg_sb(A("ones1f", [128, 1], F32, offset=SB_BASE + 210624))
    ones1b = em.reg_sb(A("ones1b", [128, 1], BF16, offset=SB_BASE + 210656))
    ones2f = em.reg_sb(A("ones2f", [1, 128], F32, offset=SB_BASE + 210688))
    selb = em.reg_sb(A("selb", [128, 32], BF16, offset=SB_BASE + 211200))
    selu8 = em.reg_sb(A("selu8", [128, 32], mybir.dt.uint8, offset=SB_BASE + 211264))

    ps0 = em.reg_ps(nc.alloc_psum_tensor("ps0", [128, 512], F32))
    ps1 = em.reg_ps(nc.alloc_psum_tensor("ps1", [128, 512], F32))
    ps2 = em.reg_ps(nc.alloc_psum_tensor("ps2", [128, 512], F32))
    ps3 = em.reg_ps(nc.alloc_psum_tensor("ps3", [128, 512], F32))
    ps4 = em.reg_ps(nc.alloc_psum_tensor("ps4", [128, 512], F32))
    ps5 = em.reg_ps(nc.alloc_psum_tensor("ps5", [128, 1024], BF16))
    ps6 = em.reg_ps(nc.alloc_psum_tensor("ps6", [128, 512], F32))
    ps7 = em.reg_ps(nc.alloc_psum_tensor("ps7", [128, 1024], BF16))

    SB = {}
    sems = {}

    def sap(t, p0, np_, c0, n, stride=1):
        F = t.shape[1]
        return bass.AP(t, p0 * F + c0, [[F, np_], [stride, n]])

    def mm(out_ap, lhsT_ap, rhs_ap, start, stop, tag="", rd=None):
        em.op("tensor",
              (lambda e, o=out_ap, l=lhsT_ap, r=rhs_ap, st=start, sp=stop:
               e.matmul(o, l, r, start=st, stop=sp)),
              reads=(rd if rd is not None else [lhsT_ap, rhs_ap]),
              writes=[out_ap], tag=tag)

    def tp(out_ap, in_ap, id_ap, tag=""):
        em.op("tensor",
              (lambda e, o=out_ap, i=in_ap, d=id_ap: e.transpose(o, i, d)),
              reads=[in_ap, id_ap], writes=[out_ap], tag=tag)

    def vop(fn, outs, ins, eng="vector", tag=""):
        em.op(eng, fn, reads=ins, writes=outs, tag=tag)

    def dma(eng, out_ap, in_ap, tag=""):
        em.op(eng, (lambda e, o=out_ap, i=in_ap: e.dma_start(out=o, in_=i)),
              reads=[in_ap], writes=[out_ap], dma=True, tag=tag)

    def tap(name, src_ap):
        if name in tap_d:
            dma("gpsimd", tap_d[name], src_ap, tag="tap_" + name)

    def xfc(c, t0, n):
        return xf[:, c * S + t0:c * S + t0 + n]

    def xnbc(c, t0, n):
        return xnb[:, c * S + t0:c * S + t0 + n]

    def ar1c(c, t0, n):
        return ar1[:, c * S + t0:c * S + t0 + n]

    for name, t in (("c_cph", cph), ("c_sph", sph), ("c_phb", phb),
                    ("c_ph2t", ph2t), ("c_sel", sel), ("c_seli", seli),
                    ("c_bneg", bneg), ("c_selb", selb), ("c_selu8", selu8), ("c_idb", idb),
                    ("c_idf", idf),
                    ("c_pblk", pblk)):
        dma("sync", t[tuple(slice(0, s) for s in t.shape)], CD[name], tag="const")
    vop(lambda e: e.memset(ones1f[:, :], 1.0), [ones1f[:, :]], [])
    vop(lambda e: e.memset(ones1b[:, :], 1.0), [ones1b[:, :]], [])
    vop(lambda e: e.memset(ones2f[:, :], 1.0), [ones2f[:, :]], [])
    vop(lambda e: e.memset(epsc[:, :], EPS), [epsc[:, :]], [])

    for ti in range(16):
        stg = stage if ti % 2 == 0 else stage2
        dma("sync", stg[:, :], X[ti * 128:(ti + 1) * 128, :], tag="in")
        for c in range(NCH):
            pst = ps4 if c % 2 == 0 else ps6
            tp(pst[:, 0:128], stg[:, c * 128:(c + 1) * 128], idf[:, :], tag="io_tp")
            vop((lambda e, o=xfc(c, ti * 128, 128), i=pst[:, 0:128]:
                 e.tensor_copy(o, i)),
                [xfc(c, ti * 128, 128)], [pst[:, 0:128]], tag="io_ev")

    def ln_stats(src_fn, src_dtype, t0, n, sqscr=None, bmb=None, brb=None):
        bmb = bmb if bmb is not None else ps2
        brb = brb if brb is not None else ps3
        """src_fn(c) -> [128, n] chunk AP at token offset t0. Computes
        bmean -> ps2, brstd -> ps3 (broadcast [128, n])."""
        o1 = ones1f if src_dtype == F32 else ones1b
        sqa, sqb = sqscr if sqscr is not None else (s3, s4)
        for c in range(NCH):
            sc = src_fn(c)
            mm(ps4[0:1, 0:n], o1[:, :], sc, c == 0, c == NCH - 1, tag="ln_sum")
            sq = (sqa if c % 2 == 0 else sqb)[:, 0:n]
            vop((lambda e, o=sq, i=sc: e.activation(o, i, AF.Square)),
                [sq], [sc], eng="scalar", tag="ln_sq")
            mm(ps6[0:1, 0:n], ones1f[:, :], sq, c == 0, c == NCH - 1, tag="ln_sum2")
        vop((lambda e, n=n: e.tensor_scalar_mul(ps4[0:1, 0:n], ps4[0:1, 0:n],
                                                1.0 / D)),
            [ps4[0:1, 0:n]], [ps4[0:1, 0:n]], tag="ln_r1")
        vop((lambda e, n=n: e.tensor_scalar_mul(ps6[0:1, 0:n], ps6[0:1, 0:n],
                                                1.0 / D)),
            [ps6[0:1, 0:n]], [ps6[0:1, 0:n]], tag="ln_r2")
        msq = rowsA[0:1, 0:512]
        vop((lambda e, o=msq, n=n: e.activation(o[:, 0:n], ps4[0:1, 0:n],
                                                AF.Square)),
            [msq], [ps4[0:1, 0:n]], eng="scalar", tag="ln_r3")
        vop((lambda e, n=n, m=msq: e.tensor_sub(ps6[0:1, 0:n], ps6[0:1, 0:n],
                                                m[:, 0:n])),
            [ps6[0:1, 0:n]], [ps6[0:1, 0:n], msq], tag="ln_r4")
        rstd = rowsA[0:1, 512:1024]
        vop((lambda e, o=rstd, n=n: e.activation(o[:, 0:n], ps6[0:1, 0:n],
                                                 AF.Sqrt, bias=epsc[0:1, 0:1])),
            [rstd], [ps6[0:1, 0:n], epsc[0:1, 0:1]], eng="scalar", tag="ln_r5")
        vop((lambda e, o=rstd, n=n: e.reciprocal(o[:, 0:n], o[:, 0:n])),
            [rstd], [rstd], tag="ln_r6")
        mr = rowsA[0:1, 0:512]
        vop((lambda e, o=mr, r=rstd, n=n: e.tensor_mul(o[:, 0:n], ps4[0:1, 0:n],
                                                       r[:, 0:n])),
            [mr], [ps4[0:1, 0:n], rstd], tag="ln_r7")
        mm(bmb[:, 0:n], ones2f[:, :], mr[:, 0:n], True, True, tag="ln_bc1")
        mm(brb[:, 0:n], ones2f[:, :], rstd[:, 0:n], True, True, tag="ln_bc2")

    def ln_to_bf(src_fn, dst_fn, t0, n, scr, bmb=None, brb=None):
        bmb = bmb if bmb is not None else ps2
        brb = brb if brb is not None else ps3
        for c in range(NCH):
            sc = src_fn(c)
            dc = dst_fn(c)
            sA = scr[:, 0:n]
            vop((lambda e, o=sA, i=sc, n=n, b=brb: e.tensor_mul(o, i, b[:, 0:n])),
                [sA], [sc, brb[:, 0:n]], tag="ln_n1")
            vop((lambda e, o=dc, i=sA, n=n, b=bmb: e.tensor_sub(o, i, b[:, 0:n])),
                [dc], [sA, bmb[:, 0:n]], tag="ln_n2")

    def ln_res_add(src_fn, src_dtype, dst_fn, t0, n, scrA, scrB, scrC,
                   bmb=None, brb=None):
        bmb = bmb if bmb is not None else ps2
        brb = brb if brb is not None else ps3
        for c in range(NCH):
            sc = src_fn(c)
            dc = dst_fn(c)
            if src_dtype == BF16:
                sf = scrC[:, 0:n]
                vop((lambda e, o=sf, i=sc: e.tensor_copy(o, i)),
                    [sf], [sc], tag="ln_a0")
            else:
                sf = sc
            a = scrA[:, 0:n]
            vop((lambda e, o=a, i=sf, n=n, b=brb: e.tensor_mul(o, i, b[:, 0:n])),
                [a], [sf, brb[:, 0:n]], tag="ln_a1")
            b_ = scrB[:, 0:n]
            vop((lambda e, o=b_, i=a, n=n, b=bmb: e.tensor_sub(o, i, b[:, 0:n])),
                [b_], [a, bmb[:, 0:n]], tag="ln_a2")
            vop((lambda e, o=dc, i=b_: e.tensor_add(o, o, i)),
                [dc], [dc, b_], tag="ln_a3")

    for tt in range(4):
        t0 = tt * 512
        ln_stats(lambda c, t0=t0: xfc(c, t0, 512), F32, t0, 512)
        ln_to_bf(lambda c, t0=t0: xfc(c, t0, 512),
                 lambda c, t0=t0: xnbc(c, t0, 512), t0, 512, s1)
    tap("xn1", xnb[:, :])

    dma("gpsimd", aab[:, 0:512].rearrange("p (kc m) -> p kc m", kc=8),
        AQ.rearrange("(kc p) m -> p kc m", p=128), tag="w_aq")
    dma("gpsimd", aab[:, 512:1024].rearrange("p (kc m) -> p kc m", kc=8),
        AV.rearrange("(kc p) m -> p kc m", p=128), tag="w_av")
    dma("gpsimd", aab[:, 1024:1536].rearrange("p (kc m) -> p kc m", kc=8),
        AO.rearrange("(kc p) m -> p kc m", p=128), tag="w_ao")
    dma("gpsimd", bbb2[0:64, 0:1024], BQ, tag="w_bq")
    dma("gpsimd", bbb2[64:128, 0:1024], BV, tag="w_bv")
    dma("gpsimd", bbb2[0:64, 1024:2048], BO, tag="w_bo")

    LQLV = 9216
    for tt in range(4):
        t0 = tt * 512
        rd_a = [aab[:, :], xnb[:, :]]
        for c in range(NCH):
            mm(ps0[0:64, 0:512], sap(aab, 0, 128, c * 64, 64),
               xnbc(c, t0, 512), c == 0, c == NCH - 1, tag="lq_mm", rd=rd_a)
        for c in range(NCH):
            mm(ps1[0:64, 0:512], sap(aab, 0, 128, 512 + c * 64, 64),
               xnbc(c, t0, 512), c == 0, c == NCH - 1, tag="lv_mm", rd=rd_a)
        vop((lambda e, o=sap(ar2, 0, 64, LQLV + t0, 512):
             e.tensor_copy(o, ps0[0:64, 0:512])),
            [sap(ar2, 0, 64, LQLV + t0, 512)], [ps0[0:64, 0:512]], tag="lq_ev")
        vop((lambda e, o=sap(ar2, 64, 64, LQLV + t0, 512):
             e.tensor_copy(o, ps1[0:64, 0:512])),
            [sap(ar2, 64, 64, LQLV + t0, 512)], [ps1[0:64, 0:512]], tag="lv_ev")

    R_AQ, R_DIFF, R_FK, R_VSM, R_WBF, R_WTS = 0, 2048, 4096, 4608, 5120, 5632
    R_WQ, R_WK, R_WV = 6144, 7168, 8192
    EBF2 = 11264
    R_FVB = 11776

    pb_aq = em.reg_sb(A("pb_aq", [128, 2048], BF16, offset=SB_BASE + 188416))
    pb_fk = em.reg_sb(A("pb_fk", [128, 512], BF16, offset=SB_BASE + 192512))
    pb_vsm = em.reg_sb(A("pb_vsm", [128, 512], BF16, offset=SB_BASE + 193536))
    pb_csp = em.reg_sb(A("pb_csp", [128, 32], F32, offset=SB_BASE + 194560))
    pb_diff = em.reg_sb(A("pb_diff", [128, 2048], BF16,
                          offset=SB_BASE + 196608))
    wts2 = em.reg_sb(A("wts2", [128, 512], BF16, offset=SB_BASE + 156672))
    s1b = em.reg_sb(A("s1b", [128, 512], BF16, offset=SB_BASE + 157696))
    s2b = em.reg_sb(A("s2b", [128, 512], BF16, offset=SB_BASE + 159744))

    PAR = [dict(aq=(ar2, R_AQ), diff=(ar2, R_DIFF), fk=(ar2, R_FK),
                vsm=(ar2, R_VSM), csp=csp),
           dict(aq=(pb_aq, 0), diff=(pb_diff, 0), fk=(pb_fk, 0),
                vsm=(pb_vsm, 0), csp=pb_csp)]

    def wcol_dma(dst_c0, Wsrc, c, tag):
        dma("gpsimd",
            sap(ar2, 0, 128, dst_c0, 1024).rearrange("p (kc m) -> p kc m", kc=8),
            Wsrc[:, c * 128:(c + 1) * 128].rearrange("(kc p) m -> p kc m", p=128),
            tag=tag)

    def pool_cums(src, fk_dst, pk_dst, tagp, peng="vector", sc0=0):
        sA = [sap(src, 0, 128, j, 128, 4) for j in range(4)]
        s3s = s3[:, sc0:sc0 + 128]
        s4s = s4[:, sc0:sc0 + 128]
        vop((lambda e, a=sA[0], b=sA[1], o=s3s: e.tensor_add(o, a, b)),
            [s3s], [src[:, :]], eng=peng, tag=tagp + "1")
        vop((lambda e, a=sA[2], b=sA[3], o=s4s: e.tensor_add(o, a, b)),
            [s4s], [src[:, :]], eng=peng, tag=tagp + "2")
        vop((lambda e, o=fk_dst, a=s3s, b=s4s: e.tensor_add(o, a, b)),
            [fk_dst], [s3s, s4s], eng=peng, tag=tagp + "3")
        pA = [sap(pk_dst, 0, 128, j, 128, 4) for j in range(4)]
        vop((lambda e, o=pA[0], a=sA[0]: e.tensor_copy(o, a)),
            [pk_dst[:, :]], [src[:, :]], eng=peng, tag=tagp + "4")
        vop((lambda e, o=pA[1], a=sA[0], b=sA[1]: e.tensor_add(o, a, b)),
            [pk_dst[:, :]], [src[:, :]], eng=peng, tag=tagp + "5")
        vop((lambda e, o=pA[2], a=pA[1], b=sA[2]: e.tensor_add(o, a, b)),
            [pk_dst[:, :]], [pk_dst[:, :], src[:, :]], eng=peng, tag=tagp + "6")
        vop((lambda e, o=pA[3], a=fk_dst: e.tensor_copy(o, a)),
            [pk_dst[:, :]], [fk_dst], eng=peng, tag=tagp + "7")

    def prep_tt(c, tt):
        P = PAR[c % 2]
        aq_t, aq_b = P["aq"]
        df_t, df_b = P["diff"]
        fk_t, fk_b = P["fk"]
        vs_t, vs_b = P["vsm"]
        cspP = P["csp"]
        t0 = tt * 512
        sl0 = tt * 128
        if tt == 0:
            wcol_dma(R_WQ, WQ, c, "w_wq")
            wcol_dma(R_WK, WK, c, "w_wk")
            wcol_dma(R_WV, WV, c, "w_wv")
        rd_q = [sap(ar2, 0, 128, R_WQ, 1024), xnb[:, :]]
        rd_k = [sap(ar2, 0, 128, R_WK, 1024), xnb[:, :]]
        rd_v = [sap(ar2, 0, 128, R_WV, 1024), xnb[:, :]]
        rd_l = [sap(bbb2, 0, 128, 0, 2048), sap(ar2, 0, 128, LQLV, 2048)]
        for kc in range(NCH):
            mm(ps0[:, 0:512], sap(ar2, 0, 128, R_WQ + kc * 128, 128),
               xnbc(kc, t0, 512), kc == 0, False, tag="q_mm", rd=rd_q)
        mm(ps0[:, 0:512], sap(bbb2, 0, 64, c * 128, 128),
           sap(ar2, 0, 64, LQLV + t0, 512), False, True, tag="q_lora", rd=rd_l)
        for kc in range(NCH):
            mm(ps1[:, 0:512], sap(ar2, 0, 128, R_WK + kc * 128, 128),
               xnbc(kc, t0, 512), kc == 0, kc == NCH - 1, tag="k_mm", rd=rd_k)
        for kc in range(NCH):
            mm(ps2[:, 0:512], sap(ar2, 0, 128, R_WV + kc * 128, 128),
               xnbc(kc, t0, 512), kc == 0, False, tag="v_mm", rd=rd_v)
        mm(ps2[:, 0:512], sap(bbb2, 64, 64, c * 128, 128),
           sap(ar2, 64, 64, LQLV + t0, 512), False, True, tag="v_lora", rd=rd_l)

        vop((lambda e: e.tensor_copy(qb[:, :], ps0[:, 0:512])),
            [qb[:, :]], [ps0[:, 0:512]], tag="q_ev")
        mm(ps0[:, 0:512], pblk[:, :], qb[:, :], True, True, tag="q_rot")
        vop((lambda e: e.tensor_copy(rb[:, :], ps0[:, 0:512])),
            [rb[:, :]], [ps0[:, 0:512]], tag="q_rev")
        vop((lambda e, t0=t0: e.tensor_mul(s1b[:, :], qb[:, :],
                                           cph[:, t0:t0 + 512])),
            [s1b[:, :]], [qb[:, :], cph[:, t0:t0 + 512]], tag="q_c")
        vop((lambda e, t0=t0: e.tensor_mul(s2b[:, :], rb[:, :],
                                           sph[:, t0:t0 + 512])),
            [s2b[:, :]], [rb[:, :], sph[:, t0:t0 + 512]], tag="q_s")
        aq_sl = sap(aq_t, 0, 128, aq_b + t0, 512)
        vop((lambda e, o=aq_sl: e.tensor_add(o, s1b[:, :], s2b[:, :])),
            [aq_sl], [s1b[:, :], s2b[:, :]], tag="q_add")

        vop((lambda e: e.tensor_copy(kb[:, :], ps1[:, 0:512])),
            [kb[:, :]], [ps1[:, 0:512]], tag="k_ev")
        mm(ps1[:, 0:512], pblk[:, :], kb[:, :], True, True, tag="k_rot")
        vop((lambda e: e.tensor_copy(rb[:, :], ps1[:, 0:512])),
            [rb[:, :]], [ps1[:, 0:512]], tag="k_rev")
        vop((lambda e, t0=t0: e.tensor_mul(s1b[:, :], kb[:, :],
                                           cph[:, t0:t0 + 512])),
            [s1b[:, :]], [kb[:, :], cph[:, t0:t0 + 512]], tag="k_c")
        vop((lambda e, t0=t0: e.tensor_mul(s2b[:, :], rb[:, :],
                                           sph[:, t0:t0 + 512])),
            [s2b[:, :]], [rb[:, :], sph[:, t0:t0 + 512]], tag="k_s")
        vop((lambda e: e.tensor_add(kshb[:, :], s1b[:, :], s2b[:, :])),
            [kshb[:, :]], [s1b[:, :], s2b[:, :]], tag="k_add")

        vop((lambda e: e.tensor_copy(vb[:, :], ps2[:, 0:512])),
            [vb[:, :]], [ps2[:, 0:512]], tag="v_ev")
        v3 = bass.AP(vb, 0, [[512, 128], [4, 128], [1, 4]])
        p3 = bass.AP(phb, 0, [[4, 128], [0, 128], [1, 4]])
        vs3 = bass.AP(vshb, 0, [[512, 128], [4, 128], [1, 4]])
        vop((lambda e, o=vs3, a=v3, b=p3: e.tensor_tensor(o, a, b, ALU.mult)),
            [vshb[:, :]], [vb[:, :], phb[:, :]], eng="gpsimd", tag="v_ph")

        fk_sl = sap(fk_t, 0, 128, fk_b + sl0, 128)
        pool_cums(kshb, fk_sl, pkb, "kp")
        if c == 0 and tt == 0:
            tap("kshb0", kshb[:, :])
            tap("pkb0", pkb[:, :])
        vop((lambda e, o=prb[:, :], a=aq_sl, b=pkb[:, :]:
             e.tensor_mul(o, a, b)),
            [prb[:, :]], [aq_sl, pkb[:, :]], tag="prod")
        if c == 0 and tt == 0:
            tap("prb0", prb[:, :])
        for st in range(4):
            ti = tt * 4 + st
            tp(ps7[:, 384:512], prb[:, st * 128:(st + 1) * 128], idb[:, :],
               tag="cs_tp")
            vop((lambda e, o=cspP[:, 2 * ti:2 * ti + 1]:
                 e.reduce_sum(o, ps7[:, 384:448], axis=AX.X)),
                [cspP[:, 2 * ti:2 * ti + 1]], [ps7[:, 384:448]], tag="cs_r0")
            vop((lambda e, o=cspP[:, 2 * ti + 1:2 * ti + 2]:
                 e.reduce_sum(o, ps7[:, 448:512], axis=AX.X)),
                [cspP[:, 2 * ti + 1:2 * ti + 2]], [ps7[:, 448:512]],
                tag="cs_r1")

        fvb = sap(ar2, 0, 128, R_FVB, 128)
        pool_cums(vshb, fvb, pvb, "vp", peng="gpsimd", sc0=128)
        dsl = [sap(df_t, 0, 128, df_b + t0 + j, 128, 4) for j in range(4)]
        dregion = sap(df_t, 0, 128, df_b + t0, 512)
        for j in range(3):
            vop((lambda e, o=dsl[j], a=sap(pvb, 0, 128, j, 128, 4), f=fvb:
                 e.tensor_sub(o, a, f)),
                [dregion], [pvb[:, :], fvb], eng="gpsimd", tag="diff")
        vop((lambda e, o=dsl[3]: e.memset(o, 0)),
            [dregion], [], eng="gpsimd", tag="diff0")
        tp(ps7[:, 256:384], fvb, idb[:, :], tag="vsm_tp")
        vop((lambda e, o=sap(vs_t, 0, 128, vs_b + sl0, 128):
             e.tensor_copy(o, ps7[:, 256:384])),
            [sap(vs_t, 0, 128, vs_b + sl0, 128)], [ps7[:, 256:384]],
            tag="vsm_ev")

    def stageA(c, i):
        P = PAR[c % 2]
        aq_t, aq_b = P["aq"]
        fk_t, fk_b = P["fk"]
        cspP = P["csp"]
        ti, h = divmod(i, 2)
        c0 = 32 * ti
        ncols = c0 + 32
        nsc = (ncols + 127) // 128
        sb = ps4 if i % 2 == 0 else ps3
        ebase = R_WBF if i % 2 == 0 else EBF2
        lhsT = sap(aq_t, h * 64, 64, aq_b + ti * 128, 128)
        rhs = sap(fk_t, h * 64, 64, fk_b, ncols)
        mm(sb[:, 0:ncols], lhsT, rhs, True, True, tag="scores",
           rd=[sap(aq_t, 0, 128, aq_b, 2048), sap(fk_t, 0, 128, fk_b, 512)])
        cscol = cspP[:, 2 * ti + h:2 * ti + h + 1]
        csb = bass.AP(cspP, 2 * ti + h, [[32, 128], [0, 32]])
        bnd = sb[:, c0:c0 + 32]
        vop((lambda e, b=bnd, m=csb: e.copy_predicated(b, selu8[:, :], m)),
            [bnd], [bnd, selu8[:, :], cscol], tag="band1")
        vop((lambda e, b=bnd: e.tensor_add(b, b, bneg[:, :])),
            [bnd], [bnd, bneg[:, :]], tag="band4")
        eccol = rows6[:, 4 + i % 2:5 + i % 2]
        vop((lambda e, o=eccol, sc=cscol: e.activation(
            o, sc, AF.Exp, scale=INV16)),
            [eccol], [cscol], eng="scalar", tag="ecx")
        ebf = sap(ar2, 0, 128, ebase, ncols)
        srow = rows6[:, i % 2:i % 2 + 1]
        vop((lambda e, o=ebf, n=ncols, s=sb, sr=srow: e.activation(
            o, s[:, 0:n], AF.Exp, scale=INV16, accum_out=sr)),
            [ebf, srow], [sb[:, 0:ncols]], eng="scalar", tag="sexp")
        if ncols % 128:
            pad = sap(ar2, 0, 128, ebase + ncols, nsc * 128 - ncols)
            vop((lambda e, o=pad: e.memset(o, 0)), [pad], [], tag="wpad")
        inv = rows6[:, 2 + i % 2:3 + i % 2]
        vop((lambda e, o=inv, sr=srow: e.reciprocal(o, sr)),
            [inv], [srow], tag="sinv")

    def stageB(c, i):
        P = PAR[c % 2]
        df_t, df_b = P["diff"]
        vs_t, vs_b = P["vsm"]
        ti, h = divmod(i, 2)
        c0 = 32 * ti
        ncols = c0 + 32
        nsc = (ncols + 127) // 128
        ebase = R_WBF if i % 2 == 0 else EBF2
        ob = ps6
        inv = rows6[:, 2 + i % 2:3 + i % 2]
        if h == 0:
            tp(ps7[:, 0:128], sap(df_t, 0, 128, df_b + ti * 128, 128),
               idb[:, :], tag="dtm_tp")
        for j in range(nsc):
            tp(ps5[:, j * 128:(j + 1) * 128],
               sap(ar2, 0, 128, ebase + j * 128, 128), idb[:, :], tag="w_tp")
        if i % 2 == 0:
            wts_t, wts_b = ar2, R_WTS
        else:
            wts_t, wts_b = wts2, 0
        wts = sap(wts_t, 0, 128, wts_b, nsc * 128)
        vop((lambda e, o=wts, n=nsc * 128: e.copy(o, ps5[:, 0:n])),
            [wts], [ps5[:, 0:nsc * 128]], eng="scalar", tag="w_ev")
        rd_vm = [sap(wts_t, 0, 128, wts_b, 512), sap(vs_t, 0, 128, vs_b, 512)]
        for j in range(nsc):
            mm(ob[:, 0:64], sap(wts_t, 0, 128, wts_b + j * 128, 128),
               sap(vs_t, 0, 128, vs_b + j * 128 + h * 64, 64),
               j == 0, j == nsc - 1, tag="vmm", rd=rd_vm)
        eccol = rows6[:, 4 + i % 2:5 + i % 2]
        sA = s64a[:, :] if i % 2 == 0 else ef[:, 0:64]
        sB = s64b[:, :] if i % 2 == 0 else ef[:, 64:128]
        vop((lambda e, h=h, o=sA, ec=eccol: e.activation(
            o, ps7[:, h * 64:h * 64 + 64], AF.Copy, scale=ec)),
            [sA], [ps7[:, 0:128], eccol], eng="scalar", tag="corr1")
        vop((lambda e, o=ob, a=sA, b=sB: e.tensor_add(b, a, o[:, 0:64])),
            [sB], [sA, ob[:, 0:64]], tag="corr2")
        vop((lambda e, h=h, iv=inv, b=sB: e.scalar_tensor_tensor(
            otm[:, h * 64:(h + 1) * 64], b, iv, ph2t[:, :],
            ALU.mult, ALU.mult)),
            [otm[:, h * 64:(h + 1) * 64]], [sB, inv, ph2t[:, :]],
            tag="ophase")
        if h == 1:
            tp(ps7[:, 128:256], otm[:, 0:128], idb[:, :], tag="o_tp")
            vop((lambda e, o=ar1c(c, ti * 128, 128): e.copy(
                o, ps7[:, 128:256])),
                [ar1c(c, ti * 128, 128)], [ps7[:, 128:256]], eng="scalar",
                tag="o_ev")

    for c in range(NCH + 1):
        for tt in range(4):
            if c < NCH:
                prep_tt(c, tt)
            if c > 0:
                if tt == 0:
                    stageA(c - 1, 0)
                    stageA(c - 1, 1)
                for i in range(tt * 8, tt * 8 + 8):
                    stageB(c - 1, i)
                    if i + 2 < 32:
                        stageA(c - 1, i + 2)
        if c == 1:
            tap("aq0", sap(ar2, 0, 128, R_AQ, S))
            tap("fullk0", sap(ar2, 0, 128, R_FK, NSLOT))
            tap("vsm0", sap(ar2, 0, 128, R_VSM, NSLOT))
            tap("diff0", sap(ar2, 0, 128, R_DIFF, S))
            tap("csp0", csp[:, :])

    tap("outfm", ar1[:, :])

    dma("gpsimd", sap(ar2, 0, 128, 0, 8192).rearrange(
        "p (kc m) -> p kc m", kc=8),
        WO.rearrange("(kc p) m -> p kc m", p=128), tag="w_wo")

    def yproj_tt(tt):
        t0 = tt * 512
        for c in range(NCH):
            mm(ps0[0:64, 0:512], sap(aab, 0, 128, 1024 + c * 64, 64),
               ar1c(c, t0, 512), c == 0, c == NCH - 1, tag="lo_mm",
               rd=[aab[:, :], ar1[:, :]])
        vop((lambda e, o=lob[:, t0:t0 + 512]: e.tensor_copy(
            o, ps0[0:64, 0:512])),
            [lob[:, t0:t0 + 512]], [ps0[0:64, 0:512]], tag="lo_ev")
        rd_y = [sap(ar2, 0, 128, 0, 8192), ar1[:, :]]
        rd_yl = [sap(bbb2, 0, 128, 0, 2048), lob[:, :]]
        for c in range(NCH):
            yps = ps1 if c % 2 == 0 else ps2
            for kc in range(NCH):
                mm(yps[:, 0:512], sap(ar2, 0, 128, kc * 1024 + c * 128, 128),
                   ar1c(kc, t0, 512), kc == 0, False, tag="y_mm", rd=rd_y)
            mm(yps[:, 0:512], sap(bbb2, 0, 64, 1024 + c * 128, 128),
               lob[:, t0:t0 + 512], False, True, tag="y_lora", rd=rd_yl)
            vop((lambda e, o=xnbc(c, t0, 512), i=yps[:, 0:512]: e.tensor_copy(
                o, i)),
                [xnbc(c, t0, 512)], [yps[:, 0:512]], tag="y_ev")

    def lnres_tt(tt):
        t0 = tt * 512
        ln_stats(lambda c, t0=t0: xnbc(c, t0, 512), BF16, t0, 512,
                 bmb=ps3, brb=ps4)
        ln_res_add(lambda c, t0=t0: xnbc(c, t0, 512), BF16,
                   lambda c, t0=t0: xfc(c, t0, 512), t0, 512, s1, s2, s4,
                   bmb=ps3, brb=ps4)

    yproj_tt(0)
    for tt in range(1, 4):
        yproj_tt(tt)
        lnres_tt(tt - 1)
    lnres_tt(3)
    tap("ybf", xnb[:, :])
    tap("x1", xf[:, :])

    for tt in range(4):
        t0 = tt * 512
        ln_stats(lambda c, t0=t0: xfc(c, t0, 512), F32, t0, 512)
        ln_to_bf(lambda c, t0=t0: xfc(c, t0, 512),
                 lambda c, t0=t0: xnbc(c, t0, 512), t0, 512, s1)

    FB = 8

    def wslice(buf, base, c0, n):
        return sap(buf, 0, 128, base + c0, n)

    for th in range(2):
        th0 = th * 1024
        pending_down = []

        def emit_down(wb_, base_d, hb, tt, e_i, fb):
            rd_d = [wslice(wb_, base_d, 0, 4096), sap(ar1, 0, 128, hb, 2048)]
            for hc in range(NCH):
                dps = ps2 if hc % 2 == 0 else ps3
                for fc in range(4):
                    mm(dps[:, 0:512],
                       wslice(wb_, base_d, fc * 1024 + hc * 128, 128),
                       sap(ar1, 0, 128, hb + fc * 512, 512),
                       fc == 0, fc == 3, tag="d_mm", rd=rd_d)
                acc_sl = accf[:, hc * 1024 + tt * 512:
                              hc * 1024 + tt * 512 + 512]
                if e_i == 0 and fb == 0:
                    vop((lambda e, o=acc_sl, i=dps[:, 0:512]:
                         e.tensor_copy(o, i)),
                        [acc_sl], [dps[:, 0:512]], tag="acc_cp")
                else:
                    vop((lambda e, o=acc_sl, i=dps[:, 0:512]:
                         e.tensor_add(o, o, i)),
                        [acc_sl], [acc_sl, dps[:, 0:512]], tag="acc_add")

        for e_i in range(NE):
            for fb in range(FB):
                par = (e_i * FB + fb) % 2
                wb_ = ar1 if par == 0 else ar2
                base_g, base_u, base_d = 0, 4096, 8192
                dma("gpsimd",
                    sap(wb_, 0, 128, base_g, 4096).rearrange(
                        "p (kc f) -> p kc f", kc=8),
                    WG[e_i, :, fb * 512:(fb + 1) * 512].rearrange(
                        "(kc p) f -> p kc f", p=128), tag="w_g")
                dma("gpsimd",
                    sap(wb_, 0, 128, base_u, 4096).rearrange(
                        "p (kc f) -> p kc f", kc=8),
                    WU[e_i, :, fb * 512:(fb + 1) * 512].rearrange(
                        "(kc p) f -> p kc f", p=128), tag="w_u")
                dma("gpsimd",
                    sap(wb_, 0, 128, base_d, 4096).rearrange(
                        "p (fc m) -> p fc m", fc=4),
                    WD_[e_i, fb * 512:(fb + 1) * 512, :].rearrange(
                        "(fc p) m -> p fc m", p=128), tag="w_d")
                for tt in range(2):
                    t0 = th0 + tt * 512
                    hb = 12288 + tt * 2048
                    for ft in range(4):
                        gps = ps0 if ft % 2 == 0 else ps4
                        ups = ps1 if ft % 2 == 0 else ps6
                        for kc in range(NCH):
                            xsl = xnb[:, kc * S + t0:kc * S + t0 + 512]
                            mm(gps[:, 0:512],
                               wslice(wb_, base_g, kc * 512 + ft * 128, 128),
                               xsl, kc == 0, kc == NCH - 1, tag="g_mm",
                               rd=[wslice(wb_, base_g, 0, 4096), xsl])
                        for kc in range(NCH):
                            xsl = xnb[:, kc * S + t0:kc * S + t0 + 512]
                            mm(ups[:, 0:512],
                               wslice(wb_, base_u, kc * 512 + ft * 128, 128),
                               xsl, kc == 0, kc == NCH - 1, tag="u_mm",
                               rd=[wslice(wb_, base_u, 0, 4096), xsl])
                        sg = fs[:, 0:512] if ft % 2 == 0 else fs[:, 512:1024]
                        vop((lambda e, o=sg, i=gps[:, 0:512]: e.activation(
                            o, i, AF.Silu)),
                            [sg], [gps[:, 0:512]], eng="scalar", tag="silu")
                        hsl = sap(ar1, 0, 128, hb + ft * 512, 512)
                        vop((lambda e, o=hsl, a=sg, b=ups[:, 0:512]:
                             e.tensor_mul(o, a, b)),
                            [hsl], [sg, ups[:, 0:512]], tag="hmul")
                        if tt == 1 and ft == 0:
                            emit_down(wb_, base_d, 12288, 0, e_i, fb)
                    if tt == 1:
                        emit_down(wb_, base_d, hb, 1, e_i, fb)
        if th == 0:
            tap("acc", accf[:, :])
        for tt in range(2):
            t0 = tt * 512
            ln_stats(lambda c, t0=t0: accf[:, c * 1024 + t0:c * 1024 + t0 + 512],
                     F32, t0, 512, sqscr=(fs[:, 0:512], fs[:, 512:1024]))
            ln_res_add(lambda c, t0=t0: accf[:, c * 1024 + t0:
                                             c * 1024 + t0 + 512],
                       F32,
                       lambda c, t0=t0, th0=th0: xfc(c, th0 + t0, 512),
                       t0, 512, fs[:, 1024:1536], fs[:, 1536:2048], None)

    for ti in range(16):
        stg = stage if ti % 2 == 0 else stage2
        for c in range(NCH):
            pst = ps4 if c % 2 == 0 else ps6
            tp(pst[:, 0:128], xfc(c, ti * 128, 128), idf[:, :], tag="ot_tp")
            vop((lambda e, o=stg[:, c * 128:(c + 1) * 128], i=pst[:, 0:128]:
                 e.tensor_copy(o, i)),
                [stg[:, c * 128:(c + 1) * 128]], [pst[:, 0:128]], tag="ot_ev")
        dma("sync", Y[ti * 128:(ti + 1) * 128, :], stg[:, :], tag="out")

    from contextlib import ExitStack
    with ExitStack() as ctx:
        for k in SEM_KEYS:
            sems[k] = ctx.enter_context(nc.semaphore(k))
        block = ctx.enter_context(nc.Block())
        em.build(block, sems, strict=strict)
    return nc, em


def kernel(L0_attn_hidden_states, L0_attn_Wq, L0_attn_Wk, L0_attn_Wv,
           L0_attn_Wo, L0_attn_Aq, L0_attn_Bq, L0_attn_Av, L0_attn_Bv,
           L0_attn_Ao, L0_attn_Bo, L0_ffn_Wg, L0_ffn_Wu, L0_ffn_Wd):
    x = np.ascontiguousarray(np.asarray(L0_attn_hidden_states, dtype=np.float32))
    consts = make_consts()
    shared = dict(
        wq=np.ascontiguousarray(np.asarray(L0_attn_Wq, np.float32)),
        wk=np.ascontiguousarray(np.asarray(L0_attn_Wk, np.float32)),
        wv=np.ascontiguousarray(np.asarray(L0_attn_Wv, np.float32)),
        wo=np.ascontiguousarray(np.asarray(L0_attn_Wo, np.float32)),
        aq=np.ascontiguousarray(np.asarray(L0_attn_Aq, np.float32)),
        bq=np.ascontiguousarray(np.asarray(L0_attn_Bq, np.float32)),
        av=np.ascontiguousarray(np.asarray(L0_attn_Av, np.float32)),
        bv=np.ascontiguousarray(np.asarray(L0_attn_Bv, np.float32)),
        ao=np.ascontiguousarray(np.asarray(L0_attn_Ao, np.float32)),
        bo=np.ascontiguousarray(np.asarray(L0_attn_Bo, np.float32)),
        wg=np.ascontiguousarray(np.asarray(L0_ffn_Wg, np.float32)),
        wu=np.ascontiguousarray(np.asarray(L0_ffn_Wu, np.float32)),
        wd=np.ascontiguousarray(np.asarray(L0_ffn_Wd, np.float32)),
        **consts,
    )
    nc, _ = build_program(strict=("vector", "scalar"))
    in_maps = [dict(x=np.ascontiguousarray(x[b]), **shared) for b in range(8)]
    res = run_bass_kernel_spmd(nc, in_maps, core_ids=list(range(8)))
    out = np.stack([r["y"] for r in res.results], axis=0).astype(np.float32)
    return out


# version 12
# speedup vs baseline: 1.1241x; 1.0010x over previous
import numpy as np
import ml_dtypes
import concourse.bass as bass
import concourse.mybir as mybir
from concourse.bass_utils import run_bass_kernel_spmd

F32 = mybir.dt.float32
BF16 = mybir.dt.bfloat16
AF = mybir.ActivationFunctionType
ALU = mybir.AluOpType
AX = mybir.AxisListType

S = 2048
D = 1024
NCH = 8
NH = 16
HD = 64
KF = 4
NSLOT = 512
LR = 64
NE = 8
FF = 4096
EPS = 1e-5
INV16 = 1.0 / 16.0

SB_BASE = 17472

ENG_SEM = {"sync": "q_sy", "gpsimd": "q_gp", "vector": "e_v", "scalar": "e_s",
           "tensor": "e_t"}
COMPUTE_SEM = {"sync": "e_sy", "gpsimd": "e_p", "vector": "e_v", "scalar": "e_s",
               "tensor": "e_t"}
DMA_POOL = 8
SEM_KEYS = (["e_v", "e_s", "e_t", "e_p", "e_sy"]
            + [f"q_sy{i}" for i in range(DMA_POOL)]
            + [f"q_gp{i}" for i in range(DMA_POOL)])


class Em:
    def __init__(self, nc):
        self.nc = nc
        self.ops = []
        self.sb_base = {}
        self.psum_id = {}
        self.writes = []
        self.reads = []

    def reg_sb(self, t):
        self.sb_base[t.name] = t.manual_sbuf_range[0]
        return t

    def reg_ps(self, t):
        self.psum_id[t.name] = len(self.psum_id)
        return t

    def _box(self, ap):
        t = ap.tensor
        tn = t.name
        if tn in self.psum_id:
            i = self.psum_id[tn]
            return (0, 128, 1 << 30 | i << 16, 1 << 30 | (i << 16) + 1)
        if tn not in self.sb_base:
            return None
        base = self.sb_base[tn]
        esz = 2 if ap.dtype == BF16 else 4
        shp = t.shape
        fbytes = 1
        for s in shp[1:]:
            fbytes *= s
        fbytes *= (2 if t.dtype == BF16 else 4)
        p0 = ap.base_partition()
        pn = ap.partition_size()
        off_b = int(ap.offset) * esz
        c0 = off_b - p0 * fbytes
        ext = 1
        pairs = [list(p) for p in ap.ap]
        for st, n in pairs[1:]:
            ext += (n - 1) * abs(st)
        ext *= esz
        return (p0, p0 + pn, base + c0, base + c0 + ext)

    @staticmethod
    def _ov(a, b):
        return a[0] < b[1] and b[0] < a[1] and a[2] < b[3] and b[2] < a[3]

    def op(self, eng, fn, reads=(), writes=(), dma=False, tag=""):
        idx = len(self.ops)
        rb = [b for b in (self._box(a) for a in reads) if b is not None]
        wb = [b for b in (self._box(a) for a in writes) if b is not None]
        deps = set()
        for b in rb:
            for e in self.writes:
                if self._ov(b, e[0]):
                    deps.add(e[1])
        for b in wb:
            for e in self.writes:
                if self._ov(b, e[0]):
                    deps.add(e[1])
            for e in self.reads:
                if self._ov(b, e[0]):
                    deps.add(e[1])
        qidx = None
        if dma:
            if not hasattr(self, "qcount"):
                self.qcount = {}
            qidx = self.qcount.get(eng, 0)
            self.qcount[eng] = qidx + 1
        self.ops.append(dict(eng=eng, fn=fn, dma=dma, deps=deps, tag=tag,
                             idx=idx, need_inc=False, qidx=qidx))
        for b in rb:
            for e in self.reads:
                if e[0] == b and self.ops[e[1]]["eng"] == eng and \
                   self.ops[e[1]]["dma"] == dma:
                    e[1] = idx
                    break
            else:
                self.reads.append([b, idx])
        for b in wb:
            for e in self.writes:
                if e[0] == b:
                    e[1] = idx
                    break
            else:
                self.writes.append([b, idx])
        return idx

    def build(self, block, sems, strict=()):
        strict = set(strict)
        ops = self.ops
        n = len(ops)
        for o in ops:
            drop = []
            for d in o["deps"]:
                dop = ops[d]
                if dop["eng"] == o["eng"] and not dop["dma"] and not o["dma"] \
                        and o["eng"] not in strict:
                    drop.append(d)
            for d in drop:
                o["deps"].discard(d)
            for d in o["deps"]:
                ops[d]["need_inc"] = True
        for o in ops:
            if o["dma"]:
                o["need_inc"] = True
        counters = {}
        prev_pool_op = {}
        for o in ops:
            if o["dma"]:
                key = ENG_SEM[o["eng"]] + str(o["qidx"] % DMA_POOL)
                rnd = o["qidx"] // DMA_POOL
                o["inc_val"] = 16 * (rnd + 1)
                counters[key] = o["inc_val"]
                o["self_wait"] = (key, 16 * rnd, prev_pool_op.get(key)) \
                    if rnd > 0 else None
                prev_pool_op[key] = o["idx"]
            else:
                key = COMPUTE_SEM[o["eng"]]
                if o["need_inc"]:
                    counters[key] = counters.get(key, 0) + 1
                    o["inc_val"] = counters[key]
                else:
                    o["inc_val"] = None
                o["self_wait"] = None
            o["sem_key"] = key
        self.totals = dict(counters)
        eng_clock = {e: {} for e in ENG_SEM}
        op_vc = [None] * n
        for o in ops:
            e = o["eng"]
            clk = eng_clock[e]
            waits = []
            if o["self_wait"] is not None:
                sk, tv, pidx = o["self_wait"]
                if clk.get(sk, 0) < tv:
                    waits.append((sk, tv))
                    if pidx is not None:
                        for k, v in op_vc[pidx].items():
                            if clk.get(k, 0) < v:
                                clk[k] = v
                    if clk.get(sk, 0) < tv:
                        clk[sk] = tv
            for d in sorted(o["deps"]):
                dop = ops[d]
                sk = dop["sem_key"]
                tv = dop["inc_val"]
                assert tv is not None
                if clk.get(sk, 0) < tv:
                    waits.append((sk, tv))
                    for k, v in op_vc[d].items():
                        if clk.get(k, 0) < v:
                            clk[k] = v
                    if clk.get(sk, 0) < tv:
                        clk[sk] = tv
            wd = {}
            for sk, tv in waits:
                wd[sk] = max(wd.get(sk, 0), tv)
            o["waits"] = list(wd.items())
            vc = dict(clk)
            if o["inc_val"] is not None:
                vc[o["sem_key"]] = o["inc_val"]
            op_vc[o["idx"]] = vc
        for eng_name in ("sync", "gpsimd", "vector", "scalar", "tensor"):
            myops = [o for o in ops if o["eng"] == eng_name]
            if not myops and eng_name != "sync":
                continue

            def body(e, myops=myops, eng_name=eng_name):
                for o in myops:
                    for sk, tv in o["waits"]:
                        e.wait_ge(sems[sk], tv)
                    inst = o["fn"](e)
                    if o["inc_val"] is not None:
                        inst.then_inc(sems[o["sem_key"]],
                                      16 if o["dma"] else 1)
                if eng_name == "sync":
                    for qk, tv in sorted(self.totals.items()):
                        if qk.startswith("q_"):
                            e.wait_ge(sems[qk], tv)
            getattr(block, eng_name)(body)


def make_consts():
    H4 = np.array([[1, 1, 1, 1], [1, -1, 1, -1], [1, 1, -1, -1], [1, -1, -1, 1]],
                  np.float64)
    inv_freq = 1.0 / 10000.0 ** (np.arange(0, HD, 2, dtype=np.float64) / HD)
    pos = np.arange(S, dtype=np.float64)
    freqs = pos[:, None] * inv_freq[None, :]
    emb = np.concatenate([freqs, freqs], axis=-1)
    cos, sin = np.cos(emb), np.sin(emb)
    phase = H4[np.arange(S) % 4][:, np.arange(HD) % 4]
    cph64 = (cos * phase).T
    sph64 = (sin * phase).T
    cph = np.concatenate([cph64, cph64], axis=0)
    sph = np.concatenate([sph64, sph64], axis=0)
    p = np.arange(128)
    phb = H4[:, p % 4].T
    r = np.arange(128)
    dd = np.arange(HD)
    ph2t = H4[r % 4][:, dd % 4]
    j32 = np.arange(32)
    sel = (j32[None, :] == (r[:, None] // 4)).astype(np.float64)
    seli = 1.0 - sel
    bneg = np.where(j32[None, :] > (r[:, None] // 4), -1e30, 0.0)
    pblk = np.zeros((128, 128))
    for a in (0, 64):
        for dp in range(32):
            pblk[a + dp + 32, a + dp] = -1.0
        for dp in range(32, 64):
            pblk[a + dp - 32, a + dp] = 1.0
    b16 = ml_dtypes.bfloat16
    return {
        "c_cph": cph.astype(np.float32), "c_sph": sph.astype(np.float32),
        "c_phb": phb.astype(b16), "c_ph2t": ph2t.astype(np.float32),
        "c_sel": sel.astype(np.float32), "c_seli": seli.astype(np.float32),
        "c_bneg": bneg.astype(np.float32),
        "c_selb": sel.astype(b16), "c_selu8": sel.astype(np.uint8),
        "c_idb": np.eye(128, dtype=b16), "c_idf": np.eye(128, dtype=np.float32),
        "c_pblk": pblk.astype(b16),
    }


def build_program(taps=(), strict=()):
    nc = bass.Bass("TRN2", target_bir_lowering=False)
    taps = set(taps)

    X = nc.dram_tensor("x", [S, D], F32, kind="ExternalInput").ap()
    Y = nc.dram_tensor("y", [S, D], F32, kind="ExternalOutput").ap()
    WQ = nc.dram_tensor("wq", [D, D], F32, kind="ExternalInput").ap()
    WK = nc.dram_tensor("wk", [D, D], F32, kind="ExternalInput").ap()
    WV = nc.dram_tensor("wv", [D, D], F32, kind="ExternalInput").ap()
    WO = nc.dram_tensor("wo", [D, D], F32, kind="ExternalInput").ap()
    AQ = nc.dram_tensor("aq", [D, LR], F32, kind="ExternalInput").ap()
    BQ = nc.dram_tensor("bq", [LR, D], F32, kind="ExternalInput").ap()
    AV = nc.dram_tensor("av", [D, LR], F32, kind="ExternalInput").ap()
    BV = nc.dram_tensor("bv", [LR, D], F32, kind="ExternalInput").ap()
    AO = nc.dram_tensor("ao", [D, LR], F32, kind="ExternalInput").ap()
    BO = nc.dram_tensor("bo", [LR, D], F32, kind="ExternalInput").ap()
    WG = nc.dram_tensor("wg", [NE, D, FF], F32, kind="ExternalInput").ap()
    WU = nc.dram_tensor("wu", [NE, D, FF], F32, kind="ExternalInput").ap()
    WD_ = nc.dram_tensor("wd", [NE, FF, D], F32, kind="ExternalInput").ap()
    CONST_SPECS = {
        "c_cph": ([128, S], F32), "c_sph": ([128, S], F32),
        "c_phb": ([128, 4], BF16), "c_ph2t": ([128, HD], F32),
        "c_sel": ([128, 32], F32), "c_seli": ([128, 32], F32),
        "c_bneg": ([128, 32], F32), "c_selb": ([128, 32], BF16),
        "c_selu8": ([128, 32], mybir.dt.uint8),
        "c_idb": ([128, 128], BF16),
        "c_idf": ([128, 128], F32), "c_pblk": ([128, 128], BF16),
    }
    CD = {k: nc.dram_tensor(k, sh, dt, kind="ExternalInput").ap()
          for k, (sh, dt) in CONST_SPECS.items()}
    tap_d = {}
    for name, sh, dt in (
        ("xn1", [128, NCH * S], F32), ("outfm", [128, NCH * S], F32),
        ("ybf", [128, NCH * S], F32), ("x1", [128, NCH * S], F32),
        ("acc", [128, NCH * 1024], F32), ("aq0", [128, S], F32),
        ("fullk0", [128, NSLOT], F32), ("vsm0", [128, NSLOT], F32),
        ("diff0", [128, S], F32), ("csp0", [128, 32], F32),
        ("kshb0", [128, 512], F32), ("pkb0", [128, 512], F32),
        ("prb0", [128, 512], F32),
    ):
        if name in taps:
            tap_d[name] = nc.dram_tensor("tap_" + name, sh, dt,
                                         kind="ExternalOutput").ap()

    em = Em(nc)
    A = nc.alloc_sbuf_tensor_at

    xf = em.reg_sb(A("xf", [128, NCH * S], F32, offset=SB_BASE + 0))
    xnb = em.reg_sb(A("xnb", [128, NCH * S], BF16, offset=SB_BASE + 65536))
    ar1 = em.reg_sb(A("ar1", [128, NCH * S], BF16, offset=SB_BASE + 98304))
    ar2 = em.reg_sb(A("ar2", [128, 12288], BF16, offset=SB_BASE + 131072))
    accf = em.reg_sb(A("accf", [128, 8192], F32, offset=SB_BASE + 155648))
    ef = em.reg_sb(A("ef", [128, 512], F32, offset=SB_BASE + 155648))
    s1 = em.reg_sb(A("s1", [128, 512], F32, offset=SB_BASE + 157696))
    s2 = em.reg_sb(A("s2", [128, 512], F32, offset=SB_BASE + 159744))
    s3 = em.reg_sb(A("s3", [128, 512], F32, offset=SB_BASE + 161792))
    s4 = em.reg_sb(A("s4", [128, 512], F32, offset=SB_BASE + 163840))
    s32 = em.reg_sb(A("s32", [128, 32], F32, offset=SB_BASE + 165888))
    s64a = em.reg_sb(A("s64a", [128, 64], F32, offset=SB_BASE + 166016))
    s64b = em.reg_sb(A("s64b", [128, 64], F32, offset=SB_BASE + 166272))
    rows6 = em.reg_sb(A("rows6", [128, 8], F32, offset=SB_BASE + 166528))
    csp = em.reg_sb(A("csp", [128, 32], F32, offset=SB_BASE + 166560))
    qb = em.reg_sb(A("qb", [128, 512], BF16, offset=SB_BASE + 166912))
    rb = em.reg_sb(A("rb", [128, 512], BF16, offset=SB_BASE + 167936))
    kb = em.reg_sb(A("kb", [128, 512], BF16, offset=SB_BASE + 168960))
    kshb = em.reg_sb(A("kshb", [128, 512], BF16, offset=SB_BASE + 169984))
    vb = em.reg_sb(A("vb", [128, 512], BF16, offset=SB_BASE + 171008))
    vshb = em.reg_sb(A("vshb", [128, 512], BF16, offset=SB_BASE + 172032))
    pkb = em.reg_sb(A("pkb", [128, 512], BF16, offset=SB_BASE + 173056))
    pvb = em.reg_sb(A("pvb", [128, 512], BF16, offset=SB_BASE + 174080))
    prb = em.reg_sb(A("prb", [128, 512], BF16, offset=SB_BASE + 175104))
    otm = em.reg_sb(A("otm", [128, 128], BF16, offset=SB_BASE + 176128))
    aab = em.reg_sb(A("aab", [128, 1536], BF16, offset=SB_BASE + 176384))
    bbb2 = em.reg_sb(A("bbb2", [128, 2048], BF16, offset=SB_BASE + 179456))
    lob = em.reg_sb(A("lob", [64, 2048], BF16, offset=SB_BASE + 183552))
    stage = em.reg_sb(A("stage", [128, 1024], F32, offset=SB_BASE + 155648))
    stage2 = em.reg_sb(A("stage2", [128, 1024], F32, offset=SB_BASE + 159744))
    fs = em.reg_sb(A("fs", [128, 2048], F32, offset=SB_BASE + 188416))
    rowsA = em.reg_sb(A("rowsA", [1, 1024], F32, offset=SB_BASE + 196608))
    cph = em.reg_sb(A("cph", [128, S], BF16, offset=SB_BASE + 200704))
    sph = em.reg_sb(A("sph", [128, S], BF16, offset=SB_BASE + 204800))
    ph2t = em.reg_sb(A("ph2t", [128, HD], F32, offset=SB_BASE + 208896))
    sel = em.reg_sb(A("sel", [128, 32], F32, offset=SB_BASE + 209152))
    seli = em.reg_sb(A("seli", [128, 32], F32, offset=SB_BASE + 209280))
    bneg = em.reg_sb(A("bneg", [128, 32], F32, offset=SB_BASE + 209408))
    idb = em.reg_sb(A("idb", [128, 128], BF16, offset=SB_BASE + 209536))
    idf = em.reg_sb(A("idf", [128, 128], F32, offset=SB_BASE + 209792))
    pblk = em.reg_sb(A("pblk", [128, 128], BF16, offset=SB_BASE + 210304))
    phb = em.reg_sb(A("phb", [128, 4], BF16, offset=SB_BASE + 210560))
    epsc = em.reg_sb(A("epsc", [128, 1], F32, offset=SB_BASE + 210592))
    ones1f = em.reg_sb(A("ones1f", [128, 1], F32, offset=SB_BASE + 210624))
    ones1b = em.reg_sb(A("ones1b", [128, 1], BF16, offset=SB_BASE + 210656))
    ones2f = em.reg_sb(A("ones2f", [1, 128], F32, offset=SB_BASE + 210688))
    selb = em.reg_sb(A("selb", [128, 32], BF16, offset=SB_BASE + 211200))
    selu8 = em.reg_sb(A("selu8", [128, 32], mybir.dt.uint8, offset=SB_BASE + 211264))

    ps0 = em.reg_ps(nc.alloc_psum_tensor("ps0", [128, 512], F32))
    ps1 = em.reg_ps(nc.alloc_psum_tensor("ps1", [128, 512], F32))
    ps2 = em.reg_ps(nc.alloc_psum_tensor("ps2", [128, 512], F32))
    ps3 = em.reg_ps(nc.alloc_psum_tensor("ps3", [128, 512], F32))
    ps4 = em.reg_ps(nc.alloc_psum_tensor("ps4", [128, 512], F32))
    ps5 = em.reg_ps(nc.alloc_psum_tensor("ps5", [128, 1024], BF16))
    ps6 = em.reg_ps(nc.alloc_psum_tensor("ps6", [128, 512], F32))
    ps7 = em.reg_ps(nc.alloc_psum_tensor("ps7", [128, 1024], BF16))

    SB = {}
    sems = {}

    def sap(t, p0, np_, c0, n, stride=1):
        F = t.shape[1]
        return bass.AP(t, p0 * F + c0, [[F, np_], [stride, n]])

    def mm(out_ap, lhsT_ap, rhs_ap, start, stop, tag="", rd=None):
        em.op("tensor",
              (lambda e, o=out_ap, l=lhsT_ap, r=rhs_ap, st=start, sp=stop:
               e.matmul(o, l, r, start=st, stop=sp)),
              reads=(rd if rd is not None else [lhsT_ap, rhs_ap]),
              writes=[out_ap], tag=tag)

    def tp(out_ap, in_ap, id_ap, tag=""):
        em.op("tensor",
              (lambda e, o=out_ap, i=in_ap, d=id_ap: e.transpose(o, i, d)),
              reads=[in_ap, id_ap], writes=[out_ap], tag=tag)

    def vop(fn, outs, ins, eng="vector", tag=""):
        em.op(eng, fn, reads=ins, writes=outs, tag=tag)

    def dma(eng, out_ap, in_ap, tag=""):
        em.op(eng, (lambda e, o=out_ap, i=in_ap: e.dma_start(out=o, in_=i)),
              reads=[in_ap], writes=[out_ap], dma=True, tag=tag)

    def tap(name, src_ap):
        if name in tap_d:
            dma("gpsimd", tap_d[name], src_ap, tag="tap_" + name)

    def xfc(c, t0, n):
        return xf[:, c * S + t0:c * S + t0 + n]

    def xnbc(c, t0, n):
        return xnb[:, c * S + t0:c * S + t0 + n]

    def ar1c(c, t0, n):
        return ar1[:, c * S + t0:c * S + t0 + n]

    for name, t in (("c_phb", phb),
                    ("c_ph2t", ph2t), ("c_sel", sel), ("c_seli", seli),
                    ("c_bneg", bneg), ("c_selb", selb), ("c_selu8", selu8), ("c_idb", idb),
                    ("c_idf", idf),
                    ("c_pblk", pblk)):
        dma("sync", t[tuple(slice(0, s) for s in t.shape)], CD[name], tag="const")
    for name, t in (("c_cph", cph), ("c_sph", sph)):
        dma("gpsimd", t[:, :], CD[name], tag="const")
    vop(lambda e: e.memset(ones1f[:, :], 1.0), [ones1f[:, :]], [])
    vop(lambda e: e.memset(ones1b[:, :], 1.0), [ones1b[:, :]], [])
    vop(lambda e: e.memset(ones2f[:, :], 1.0), [ones2f[:, :]], [])
    vop(lambda e: e.memset(epsc[:, :], EPS), [epsc[:, :]], [])

    for ti in range(16):
        stg = stage if ti % 2 == 0 else stage2
        dma("sync", stg[:, :], X[ti * 128:(ti + 1) * 128, :], tag="in")
        for c in range(NCH):
            pst = ps4 if c % 2 == 0 else ps6
            tp(pst[:, 0:128], stg[:, c * 128:(c + 1) * 128], idf[:, :], tag="io_tp")
            vop((lambda e, o=xfc(c, ti * 128, 128), i=pst[:, 0:128]:
                 e.tensor_copy(o, i)),
                [xfc(c, ti * 128, 128)], [pst[:, 0:128]], tag="io_ev")

    def ln_stats(src_fn, src_dtype, t0, n, sqscr=None, bmb=None, brb=None):
        bmb = bmb if bmb is not None else ps2
        brb = brb if brb is not None else ps3
        """src_fn(c) -> [128, n] chunk AP at token offset t0. Computes
        bmean -> ps2, brstd -> ps3 (broadcast [128, n])."""
        o1 = ones1f if src_dtype == F32 else ones1b
        sqa, sqb = sqscr if sqscr is not None else (s3, s4)
        for c in range(NCH):
            sc = src_fn(c)
            mm(ps4[0:1, 0:n], o1[:, :], sc, c == 0, c == NCH - 1, tag="ln_sum")
            sq = (sqa if c % 2 == 0 else sqb)[:, 0:n]
            vop((lambda e, o=sq, i=sc: e.activation(o, i, AF.Square)),
                [sq], [sc], eng="scalar", tag="ln_sq")
            mm(ps6[0:1, 0:n], ones1f[:, :], sq, c == 0, c == NCH - 1, tag="ln_sum2")
        vop((lambda e, n=n: e.tensor_scalar_mul(ps4[0:1, 0:n], ps4[0:1, 0:n],
                                                1.0 / D)),
            [ps4[0:1, 0:n]], [ps4[0:1, 0:n]], tag="ln_r1")
        vop((lambda e, n=n: e.tensor_scalar_mul(ps6[0:1, 0:n], ps6[0:1, 0:n],
                                                1.0 / D)),
            [ps6[0:1, 0:n]], [ps6[0:1, 0:n]], tag="ln_r2")
        msq = rowsA[0:1, 0:512]
        vop((lambda e, o=msq, n=n: e.activation(o[:, 0:n], ps4[0:1, 0:n],
                                                AF.Square)),
            [msq], [ps4[0:1, 0:n]], eng="scalar", tag="ln_r3")
        vop((lambda e, n=n, m=msq: e.tensor_sub(ps6[0:1, 0:n], ps6[0:1, 0:n],
                                                m[:, 0:n])),
            [ps6[0:1, 0:n]], [ps6[0:1, 0:n], msq], tag="ln_r4")
        rstd = rowsA[0:1, 512:1024]
        vop((lambda e, o=rstd, n=n: e.activation(o[:, 0:n], ps6[0:1, 0:n],
                                                 AF.Sqrt, bias=epsc[0:1, 0:1])),
            [rstd], [ps6[0:1, 0:n], epsc[0:1, 0:1]], eng="scalar", tag="ln_r5")
        vop((lambda e, o=rstd, n=n: e.reciprocal(o[:, 0:n], o[:, 0:n])),
            [rstd], [rstd], tag="ln_r6")
        mr = rowsA[0:1, 0:512]
        vop((lambda e, o=mr, r=rstd, n=n: e.tensor_mul(o[:, 0:n], ps4[0:1, 0:n],
                                                       r[:, 0:n])),
            [mr], [ps4[0:1, 0:n], rstd], tag="ln_r7")
        mm(bmb[:, 0:n], ones2f[:, :], mr[:, 0:n], True, True, tag="ln_bc1")
        mm(brb[:, 0:n], ones2f[:, :], rstd[:, 0:n], True, True, tag="ln_bc2")

    def ln_to_bf(src_fn, dst_fn, t0, n, scr, bmb=None, brb=None):
        bmb = bmb if bmb is not None else ps2
        brb = brb if brb is not None else ps3
        for c in range(NCH):
            sc = src_fn(c)
            dc = dst_fn(c)
            sA = scr[:, 0:n]
            vop((lambda e, o=sA, i=sc, n=n, b=brb: e.tensor_mul(o, i, b[:, 0:n])),
                [sA], [sc, brb[:, 0:n]], tag="ln_n1")
            vop((lambda e, o=dc, i=sA, n=n, b=bmb: e.tensor_sub(o, i, b[:, 0:n])),
                [dc], [sA, bmb[:, 0:n]], tag="ln_n2")

    def ln_res_add(src_fn, src_dtype, dst_fn, t0, n, scrA, scrB, scrC,
                   bmb=None, brb=None):
        bmb = bmb if bmb is not None else ps2
        brb = brb if brb is not None else ps3
        for c in range(NCH):
            sc = src_fn(c)
            dc = dst_fn(c)
            if src_dtype == BF16:
                sf = scrC[:, 0:n]
                vop((lambda e, o=sf, i=sc: e.tensor_copy(o, i)),
                    [sf], [sc], tag="ln_a0")
            else:
                sf = sc
            a = scrA[:, 0:n]
            vop((lambda e, o=a, i=sf, n=n, b=brb: e.tensor_mul(o, i, b[:, 0:n])),
                [a], [sf, brb[:, 0:n]], tag="ln_a1")
            b_ = scrB[:, 0:n]
            vop((lambda e, o=b_, i=a, n=n, b=bmb: e.tensor_sub(o, i, b[:, 0:n])),
                [b_], [a, bmb[:, 0:n]], tag="ln_a2")
            vop((lambda e, o=dc, i=b_: e.tensor_add(o, o, i)),
                [dc], [dc, b_], tag="ln_a3")

    for tt in range(4):
        t0 = tt * 512
        ln_stats(lambda c, t0=t0: xfc(c, t0, 512), F32, t0, 512)
        ln_to_bf(lambda c, t0=t0: xfc(c, t0, 512),
                 lambda c, t0=t0: xnbc(c, t0, 512), t0, 512, s1)
    tap("xn1", xnb[:, :])

    dma("gpsimd", aab[:, 0:512].rearrange("p (kc m) -> p kc m", kc=8),
        AQ.rearrange("(kc p) m -> p kc m", p=128), tag="w_aq")
    dma("gpsimd", aab[:, 512:1024].rearrange("p (kc m) -> p kc m", kc=8),
        AV.rearrange("(kc p) m -> p kc m", p=128), tag="w_av")
    dma("gpsimd", aab[:, 1024:1536].rearrange("p (kc m) -> p kc m", kc=8),
        AO.rearrange("(kc p) m -> p kc m", p=128), tag="w_ao")
    dma("gpsimd", bbb2[0:64, 0:1024], BQ, tag="w_bq")
    dma("gpsimd", bbb2[64:128, 0:1024], BV, tag="w_bv")
    dma("gpsimd", bbb2[0:64, 1024:2048], BO, tag="w_bo")

    LQLV = 9216
    for tt in range(4):
        t0 = tt * 512
        rd_a = [aab[:, :], xnb[:, :]]
        for c in range(NCH):
            mm(ps0[0:64, 0:512], sap(aab, 0, 128, c * 64, 64),
               xnbc(c, t0, 512), c == 0, c == NCH - 1, tag="lq_mm", rd=rd_a)
        for c in range(NCH):
            mm(ps1[0:64, 0:512], sap(aab, 0, 128, 512 + c * 64, 64),
               xnbc(c, t0, 512), c == 0, c == NCH - 1, tag="lv_mm", rd=rd_a)
        vop((lambda e, o=sap(ar2, 0, 64, LQLV + t0, 512):
             e.tensor_copy(o, ps0[0:64, 0:512])),
            [sap(ar2, 0, 64, LQLV + t0, 512)], [ps0[0:64, 0:512]], tag="lq_ev")
        vop((lambda e, o=sap(ar2, 64, 64, LQLV + t0, 512):
             e.tensor_copy(o, ps1[0:64, 0:512])),
            [sap(ar2, 64, 64, LQLV + t0, 512)], [ps1[0:64, 0:512]], tag="lv_ev")

    R_AQ, R_DIFF, R_FK, R_VSM, R_WBF, R_WTS = 0, 2048, 4096, 4608, 5120, 5632
    R_WQ, R_WK, R_WV = 6144, 7168, 8192
    EBF2 = 11264
    R_FVB = 11776

    pb_aq = em.reg_sb(A("pb_aq", [128, 2048], BF16, offset=SB_BASE + 188416))
    pb_fk = em.reg_sb(A("pb_fk", [128, 512], BF16, offset=SB_BASE + 192512))
    pb_vsm = em.reg_sb(A("pb_vsm", [128, 512], BF16, offset=SB_BASE + 193536))
    pb_csp = em.reg_sb(A("pb_csp", [128, 32], F32, offset=SB_BASE + 194560))
    pb_diff = em.reg_sb(A("pb_diff", [128, 2048], BF16,
                          offset=SB_BASE + 196608))
    wts2 = em.reg_sb(A("wts2", [128, 512], BF16, offset=SB_BASE + 156672))
    s1b = em.reg_sb(A("s1b", [128, 512], BF16, offset=SB_BASE + 157696))
    s2b = em.reg_sb(A("s2b", [128, 512], BF16, offset=SB_BASE + 159744))

    PAR = [dict(aq=(ar2, R_AQ), diff=(ar2, R_DIFF), fk=(ar2, R_FK),
                vsm=(ar2, R_VSM), csp=csp),
           dict(aq=(pb_aq, 0), diff=(pb_diff, 0), fk=(pb_fk, 0),
                vsm=(pb_vsm, 0), csp=pb_csp)]

    def wcol_dma(dst_c0, Wsrc, c, tag):
        dma("gpsimd",
            sap(ar2, 0, 128, dst_c0, 1024).rearrange("p (kc m) -> p kc m", kc=8),
            Wsrc[:, c * 128:(c + 1) * 128].rearrange("(kc p) m -> p kc m", p=128),
            tag=tag)

    def pool_cums(src, fk_dst, pk_dst, tagp, peng="vector", sc0=0):
        sA = [sap(src, 0, 128, j, 128, 4) for j in range(4)]
        s3s = s3[:, sc0:sc0 + 128]
        s4s = s4[:, sc0:sc0 + 128]
        vop((lambda e, a=sA[0], b=sA[1], o=s3s: e.tensor_add(o, a, b)),
            [s3s], [src[:, :]], eng=peng, tag=tagp + "1")
        vop((lambda e, a=sA[2], b=sA[3], o=s4s: e.tensor_add(o, a, b)),
            [s4s], [src[:, :]], eng=peng, tag=tagp + "2")
        vop((lambda e, o=fk_dst, a=s3s, b=s4s: e.tensor_add(o, a, b)),
            [fk_dst], [s3s, s4s], eng=peng, tag=tagp + "3")
        pA = [sap(pk_dst, 0, 128, j, 128, 4) for j in range(4)]
        vop((lambda e, o=pA[0], a=sA[0]: e.tensor_copy(o, a)),
            [pk_dst[:, :]], [src[:, :]], eng=peng, tag=tagp + "4")
        vop((lambda e, o=pA[1], a=sA[0], b=sA[1]: e.tensor_add(o, a, b)),
            [pk_dst[:, :]], [src[:, :]], eng=peng, tag=tagp + "5")
        vop((lambda e, o=pA[2], a=pA[1], b=sA[2]: e.tensor_add(o, a, b)),
            [pk_dst[:, :]], [pk_dst[:, :], src[:, :]], eng=peng, tag=tagp + "6")
        vop((lambda e, o=pA[3], a=fk_dst: e.tensor_copy(o, a)),
            [pk_dst[:, :]], [fk_dst], eng=peng, tag=tagp + "7")

    def prep_tt(c, tt):
        P = PAR[c % 2]
        aq_t, aq_b = P["aq"]
        df_t, df_b = P["diff"]
        fk_t, fk_b = P["fk"]
        vs_t, vs_b = P["vsm"]
        cspP = P["csp"]
        t0 = tt * 512
        sl0 = tt * 128
        if tt == 0:
            wcol_dma(R_WQ, WQ, c, "w_wq")
            wcol_dma(R_WK, WK, c, "w_wk")
            wcol_dma(R_WV, WV, c, "w_wv")
        rd_q = [sap(ar2, 0, 128, R_WQ, 1024), xnb[:, :]]
        rd_k = [sap(ar2, 0, 128, R_WK, 1024), xnb[:, :]]
        rd_v = [sap(ar2, 0, 128, R_WV, 1024), xnb[:, :]]
        rd_l = [sap(bbb2, 0, 128, 0, 2048), sap(ar2, 0, 128, LQLV, 2048)]
        for kc in range(NCH):
            mm(ps0[:, 0:512], sap(ar2, 0, 128, R_WQ + kc * 128, 128),
               xnbc(kc, t0, 512), kc == 0, False, tag="q_mm", rd=rd_q)
        mm(ps0[:, 0:512], sap(bbb2, 0, 64, c * 128, 128),
           sap(ar2, 0, 64, LQLV + t0, 512), False, True, tag="q_lora", rd=rd_l)
        for kc in range(NCH):
            mm(ps1[:, 0:512], sap(ar2, 0, 128, R_WK + kc * 128, 128),
               xnbc(kc, t0, 512), kc == 0, kc == NCH - 1, tag="k_mm", rd=rd_k)
        for kc in range(NCH):
            mm(ps2[:, 0:512], sap(ar2, 0, 128, R_WV + kc * 128, 128),
               xnbc(kc, t0, 512), kc == 0, False, tag="v_mm", rd=rd_v)
        mm(ps2[:, 0:512], sap(bbb2, 64, 64, c * 128, 128),
           sap(ar2, 64, 64, LQLV + t0, 512), False, True, tag="v_lora", rd=rd_l)

        vop((lambda e: e.tensor_copy(qb[:, :], ps0[:, 0:512])),
            [qb[:, :]], [ps0[:, 0:512]], tag="q_ev")
        mm(ps0[:, 0:512], pblk[:, :], qb[:, :], True, True, tag="q_rot")
        vop((lambda e: e.tensor_copy(rb[:, :], ps0[:, 0:512])),
            [rb[:, :]], [ps0[:, 0:512]], tag="q_rev")
        vop((lambda e, t0=t0: e.tensor_mul(s1b[:, :], qb[:, :],
                                           cph[:, t0:t0 + 512])),
            [s1b[:, :]], [qb[:, :], cph[:, t0:t0 + 512]], tag="q_c")
        vop((lambda e, t0=t0: e.tensor_mul(s2b[:, :], rb[:, :],
                                           sph[:, t0:t0 + 512])),
            [s2b[:, :]], [rb[:, :], sph[:, t0:t0 + 512]], tag="q_s")
        aq_sl = sap(aq_t, 0, 128, aq_b + t0, 512)
        vop((lambda e, o=aq_sl: e.tensor_add(o, s1b[:, :], s2b[:, :])),
            [aq_sl], [s1b[:, :], s2b[:, :]], tag="q_add")

        vop((lambda e: e.tensor_copy(kb[:, :], ps1[:, 0:512])),
            [kb[:, :]], [ps1[:, 0:512]], tag="k_ev")
        mm(ps1[:, 0:512], pblk[:, :], kb[:, :], True, True, tag="k_rot")
        vop((lambda e: e.tensor_copy(rb[:, :], ps1[:, 0:512])),
            [rb[:, :]], [ps1[:, 0:512]], tag="k_rev")
        vop((lambda e, t0=t0: e.tensor_mul(s1b[:, :], kb[:, :],
                                           cph[:, t0:t0 + 512])),
            [s1b[:, :]], [kb[:, :], cph[:, t0:t0 + 512]], tag="k_c")
        vop((lambda e, t0=t0: e.tensor_mul(s2b[:, :], rb[:, :],
                                           sph[:, t0:t0 + 512])),
            [s2b[:, :]], [rb[:, :], sph[:, t0:t0 + 512]], tag="k_s")
        vop((lambda e: e.tensor_add(kshb[:, :], s1b[:, :], s2b[:, :])),
            [kshb[:, :]], [s1b[:, :], s2b[:, :]], tag="k_add")

        vop((lambda e: e.tensor_copy(vb[:, :], ps2[:, 0:512])),
            [vb[:, :]], [ps2[:, 0:512]], tag="v_ev")
        v3 = bass.AP(vb, 0, [[512, 128], [4, 128], [1, 4]])
        p3 = bass.AP(phb, 0, [[4, 128], [0, 128], [1, 4]])
        vs3 = bass.AP(vshb, 0, [[512, 128], [4, 128], [1, 4]])
        vop((lambda e, o=vs3, a=v3, b=p3: e.tensor_tensor(o, a, b, ALU.mult)),
            [vshb[:, :]], [vb[:, :], phb[:, :]], eng="gpsimd", tag="v_ph")

        fk_sl = sap(fk_t, 0, 128, fk_b + sl0, 128)
        pool_cums(kshb, fk_sl, pkb, "kp")
        if c == 0 and tt == 0:
            tap("kshb0", kshb[:, :])
            tap("pkb0", pkb[:, :])
        vop((lambda e, o=prb[:, :], a=aq_sl, b=pkb[:, :]:
             e.tensor_mul(o, a, b)),
            [prb[:, :]], [aq_sl, pkb[:, :]], tag="prod")
        if c == 0 and tt == 0:
            tap("prb0", prb[:, :])
        for st in range(4):
            ti = tt * 4 + st
            tp(ps7[:, 384:512], prb[:, st * 128:(st + 1) * 128], idb[:, :],
               tag="cs_tp")
            vop((lambda e, o=cspP[:, 2 * ti:2 * ti + 1]:
                 e.reduce_sum(o, ps7[:, 384:448], axis=AX.X)),
                [cspP[:, 2 * ti:2 * ti + 1]], [ps7[:, 384:448]], tag="cs_r0")
            vop((lambda e, o=cspP[:, 2 * ti + 1:2 * ti + 2]:
                 e.reduce_sum(o, ps7[:, 448:512], axis=AX.X)),
                [cspP[:, 2 * ti + 1:2 * ti + 2]], [ps7[:, 448:512]],
                tag="cs_r1")

        fvb = sap(ar2, 0, 128, R_FVB, 128)
        pool_cums(vshb, fvb, pvb, "vp", peng="gpsimd", sc0=128)
        dsl = [sap(df_t, 0, 128, df_b + t0 + j, 128, 4) for j in range(4)]
        dregion = sap(df_t, 0, 128, df_b + t0, 512)
        for j in range(3):
            vop((lambda e, o=dsl[j], a=sap(pvb, 0, 128, j, 128, 4), f=fvb:
                 e.tensor_sub(o, a, f)),
                [dregion], [pvb[:, :], fvb], eng="gpsimd", tag="diff")
        vop((lambda e, o=dsl[3]: e.memset(o, 0)),
            [dregion], [], eng="gpsimd", tag="diff0")
        tp(ps7[:, 256:384], fvb, idb[:, :], tag="vsm_tp")
        vop((lambda e, o=sap(vs_t, 0, 128, vs_b + sl0, 128):
             e.tensor_copy(o, ps7[:, 256:384])),
            [sap(vs_t, 0, 128, vs_b + sl0, 128)], [ps7[:, 256:384]],
            tag="vsm_ev")

    def stageA(c, i):
        P = PAR[c % 2]
        aq_t, aq_b = P["aq"]
        fk_t, fk_b = P["fk"]
        cspP = P["csp"]
        ti, h = divmod(i, 2)
        c0 = 32 * ti
        ncols = c0 + 32
        nsc = (ncols + 127) // 128
        sb = ps4 if i % 2 == 0 else ps3
        ebase = R_WBF if i % 2 == 0 else EBF2
        lhsT = sap(aq_t, h * 64, 64, aq_b + ti * 128, 128)
        rhs = sap(fk_t, h * 64, 64, fk_b, ncols)
        mm(sb[:, 0:ncols], lhsT, rhs, True, True, tag="scores",
           rd=[sap(aq_t, 0, 128, aq_b, 2048), sap(fk_t, 0, 128, fk_b, 512)])
        cscol = cspP[:, 2 * ti + h:2 * ti + h + 1]
        csb = bass.AP(cspP, 2 * ti + h, [[32, 128], [0, 32]])
        bnd = sb[:, c0:c0 + 32]
        vop((lambda e, b=bnd, m=csb: e.copy_predicated(b, selu8[:, :], m)),
            [bnd], [bnd, selu8[:, :], cscol], tag="band1")
        vop((lambda e, b=bnd: e.tensor_add(b, b, bneg[:, :])),
            [bnd], [bnd, bneg[:, :]], tag="band4")
        eccol = rows6[:, 4 + i % 2:5 + i % 2]
        vop((lambda e, o=eccol, sc=cscol: e.activation(
            o, sc, AF.Exp, scale=INV16)),
            [eccol], [cscol], eng="scalar", tag="ecx")
        ebf = sap(ar2, 0, 128, ebase, ncols)
        srow = rows6[:, i % 2:i % 2 + 1]
        vop((lambda e, o=ebf, n=ncols, s=sb, sr=srow: e.activation(
            o, s[:, 0:n], AF.Exp, scale=INV16, accum_out=sr)),
            [ebf, srow], [sb[:, 0:ncols]], eng="scalar", tag="sexp")
        if ncols % 128:
            pad = sap(ar2, 0, 128, ebase + ncols, nsc * 128 - ncols)
            vop((lambda e, o=pad: e.memset(o, 0)), [pad], [], tag="wpad")
        inv = rows6[:, 2 + i % 2:3 + i % 2]
        vop((lambda e, o=inv, sr=srow: e.reciprocal(o, sr)),
            [inv], [srow], tag="sinv")

    def stageB(c, i):
        P = PAR[c % 2]
        df_t, df_b = P["diff"]
        vs_t, vs_b = P["vsm"]
        ti, h = divmod(i, 2)
        c0 = 32 * ti
        ncols = c0 + 32
        nsc = (ncols + 127) // 128
        ebase = R_WBF if i % 2 == 0 else EBF2
        ob = ps6
        inv = rows6[:, 2 + i % 2:3 + i % 2]
        if h == 0:
            tp(ps7[:, 0:128], sap(df_t, 0, 128, df_b + ti * 128, 128),
               idb[:, :], tag="dtm_tp")
        for j in range(nsc):
            tp(ps5[:, j * 128:(j + 1) * 128],
               sap(ar2, 0, 128, ebase + j * 128, 128), idb[:, :], tag="w_tp")
        if i % 2 == 0:
            wts_t, wts_b = ar2, R_WTS
        else:
            wts_t, wts_b = wts2, 0
        wts = sap(wts_t, 0, 128, wts_b, nsc * 128)
        vop((lambda e, o=wts, n=nsc * 128: e.copy(o, ps5[:, 0:n])),
            [wts], [ps5[:, 0:nsc * 128]], eng="scalar", tag="w_ev")
        rd_vm = [sap(wts_t, 0, 128, wts_b, 512), sap(vs_t, 0, 128, vs_b, 512)]
        for j in range(nsc):
            mm(ob[:, 0:64], sap(wts_t, 0, 128, wts_b + j * 128, 128),
               sap(vs_t, 0, 128, vs_b + j * 128 + h * 64, 64),
               j == 0, j == nsc - 1, tag="vmm", rd=rd_vm)
        eccol = rows6[:, 4 + i % 2:5 + i % 2]
        sA = s64a[:, :] if i % 2 == 0 else ef[:, 0:64]
        sB = s64b[:, :] if i % 2 == 0 else ef[:, 64:128]
        vop((lambda e, h=h, o=sA, ec=eccol: e.activation(
            o, ps7[:, h * 64:h * 64 + 64], AF.Copy, scale=ec)),
            [sA], [ps7[:, 0:128], eccol], eng="scalar", tag="corr1")
        vop((lambda e, o=ob, a=sA, b=sB: e.tensor_add(b, a, o[:, 0:64])),
            [sB], [sA, ob[:, 0:64]], tag="corr2")
        vop((lambda e, h=h, iv=inv, b=sB: e.scalar_tensor_tensor(
            otm[:, h * 64:(h + 1) * 64], b, iv, ph2t[:, :],
            ALU.mult, ALU.mult)),
            [otm[:, h * 64:(h + 1) * 64]], [sB, inv, ph2t[:, :]],
            tag="ophase")
        if h == 1:
            tp(ps7[:, 128:256], otm[:, 0:128], idb[:, :], tag="o_tp")
            vop((lambda e, o=ar1c(c, ti * 128, 128): e.copy(
                o, ps7[:, 128:256])),
                [ar1c(c, ti * 128, 128)], [ps7[:, 128:256]], eng="scalar",
                tag="o_ev")

    for c in range(NCH + 1):
        for tt in range(4):
            if c < NCH:
                prep_tt(c, tt)
            if c > 0:
                if tt == 0:
                    stageA(c - 1, 0)
                    stageA(c - 1, 1)
                for i in range(tt * 8, tt * 8 + 8):
                    stageB(c - 1, i)
                    if i + 2 < 32:
                        stageA(c - 1, i + 2)
        if c == 1:
            tap("aq0", sap(ar2, 0, 128, R_AQ, S))
            tap("fullk0", sap(ar2, 0, 128, R_FK, NSLOT))
            tap("vsm0", sap(ar2, 0, 128, R_VSM, NSLOT))
            tap("diff0", sap(ar2, 0, 128, R_DIFF, S))
            tap("csp0", csp[:, :])

    tap("outfm", ar1[:, :])

    dma("gpsimd", sap(ar2, 0, 128, 0, 8192).rearrange(
        "p (kc m) -> p kc m", kc=8),
        WO.rearrange("(kc p) m -> p kc m", p=128), tag="w_wo")

    def yproj_tt(tt):
        t0 = tt * 512
        for c in range(NCH):
            mm(ps0[0:64, 0:512], sap(aab, 0, 128, 1024 + c * 64, 64),
               ar1c(c, t0, 512), c == 0, c == NCH - 1, tag="lo_mm",
               rd=[aab[:, :], ar1[:, :]])
        vop((lambda e, o=lob[:, t0:t0 + 512]: e.tensor_copy(
            o, ps0[0:64, 0:512])),
            [lob[:, t0:t0 + 512]], [ps0[0:64, 0:512]], tag="lo_ev")
        rd_y = [sap(ar2, 0, 128, 0, 8192), ar1[:, :]]
        rd_yl = [sap(bbb2, 0, 128, 0, 2048), lob[:, :]]
        for c in range(NCH):
            yps = ps1 if c % 2 == 0 else ps2
            for kc in range(NCH):
                mm(yps[:, 0:512], sap(ar2, 0, 128, kc * 1024 + c * 128, 128),
                   ar1c(kc, t0, 512), kc == 0, False, tag="y_mm", rd=rd_y)
            mm(yps[:, 0:512], sap(bbb2, 0, 64, 1024 + c * 128, 128),
               lob[:, t0:t0 + 512], False, True, tag="y_lora", rd=rd_yl)
            vop((lambda e, o=xnbc(c, t0, 512), i=yps[:, 0:512]: e.tensor_copy(
                o, i)),
                [xnbc(c, t0, 512)], [yps[:, 0:512]], tag="y_ev")

    def lnres_tt(tt):
        t0 = tt * 512
        ln_stats(lambda c, t0=t0: xnbc(c, t0, 512), BF16, t0, 512,
                 bmb=ps3, brb=ps4)
        ln_res_add(lambda c, t0=t0: xnbc(c, t0, 512), BF16,
                   lambda c, t0=t0: xfc(c, t0, 512), t0, 512, s1, s2, s4,
                   bmb=ps3, brb=ps4)

    yproj_tt(0)
    for tt in range(1, 4):
        yproj_tt(tt)
        lnres_tt(tt - 1)
    lnres_tt(3)
    tap("ybf", xnb[:, :])
    tap("x1", xf[:, :])

    for tt in range(4):
        t0 = tt * 512
        ln_stats(lambda c, t0=t0: xfc(c, t0, 512), F32, t0, 512)
        ln_to_bf(lambda c, t0=t0: xfc(c, t0, 512),
                 lambda c, t0=t0: xnbc(c, t0, 512), t0, 512, s1)

    FB = 8

    def wslice(buf, base, c0, n):
        return sap(buf, 0, 128, base + c0, n)

    for th in range(2):
        th0 = th * 1024
        pending_down = []

        def emit_down(wb_, base_d, hb, tt, e_i, fb):
            rd_d = [wslice(wb_, base_d, 0, 4096), sap(ar1, 0, 128, hb, 2048)]
            for hc in range(NCH):
                dps = ps2 if hc % 2 == 0 else ps3
                for fc in range(4):
                    mm(dps[:, 0:512],
                       wslice(wb_, base_d, fc * 1024 + hc * 128, 128),
                       sap(ar1, 0, 128, hb + fc * 512, 512),
                       fc == 0, fc == 3, tag="d_mm", rd=rd_d)
                acc_sl = accf[:, hc * 1024 + tt * 512:
                              hc * 1024 + tt * 512 + 512]
                if e_i == 0 and fb == 0:
                    vop((lambda e, o=acc_sl, i=dps[:, 0:512]:
                         e.tensor_copy(o, i)),
                        [acc_sl], [dps[:, 0:512]], tag="acc_cp")
                else:
                    vop((lambda e, o=acc_sl, i=dps[:, 0:512]:
                         e.tensor_add(o, o, i)),
                        [acc_sl], [acc_sl, dps[:, 0:512]], tag="acc_add")

        for e_i in range(NE):
            for fb in range(FB):
                par = (e_i * FB + fb) % 2
                wb_ = ar1 if par == 0 else ar2
                base_g, base_u, base_d = 0, 4096, 8192
                dma("gpsimd",
                    sap(wb_, 0, 128, base_g, 4096).rearrange(
                        "p (kc f) -> p kc f", kc=8),
                    WG[e_i, :, fb * 512:(fb + 1) * 512].rearrange(
                        "(kc p) f -> p kc f", p=128), tag="w_g")
                dma("gpsimd",
                    sap(wb_, 0, 128, base_u, 4096).rearrange(
                        "p (kc f) -> p kc f", kc=8),
                    WU[e_i, :, fb * 512:(fb + 1) * 512].rearrange(
                        "(kc p) f -> p kc f", p=128), tag="w_u")
                dma("gpsimd",
                    sap(wb_, 0, 128, base_d, 4096).rearrange(
                        "p (fc m) -> p fc m", fc=4),
                    WD_[e_i, fb * 512:(fb + 1) * 512, :].rearrange(
                        "(fc p) m -> p fc m", p=128), tag="w_d")
                for tt in range(2):
                    t0 = th0 + tt * 512
                    hb = 12288 + tt * 2048
                    for ft in range(4):
                        gps = ps0 if ft % 2 == 0 else ps4
                        ups = ps1 if ft % 2 == 0 else ps6
                        for kc in range(NCH):
                            xsl = xnb[:, kc * S + t0:kc * S + t0 + 512]
                            mm(gps[:, 0:512],
                               wslice(wb_, base_g, kc * 512 + ft * 128, 128),
                               xsl, kc == 0, kc == NCH - 1, tag="g_mm",
                               rd=[wslice(wb_, base_g, 0, 4096), xsl])
                        for kc in range(NCH):
                            xsl = xnb[:, kc * S + t0:kc * S + t0 + 512]
                            mm(ups[:, 0:512],
                               wslice(wb_, base_u, kc * 512 + ft * 128, 128),
                               xsl, kc == 0, kc == NCH - 1, tag="u_mm",
                               rd=[wslice(wb_, base_u, 0, 4096), xsl])
                        sg = fs[:, 0:512] if ft % 2 == 0 else fs[:, 512:1024]
                        vop((lambda e, o=sg, i=gps[:, 0:512]: e.activation(
                            o, i, AF.Silu)),
                            [sg], [gps[:, 0:512]], eng="scalar", tag="silu")
                        hsl = sap(ar1, 0, 128, hb + ft * 512, 512)
                        vop((lambda e, o=hsl, a=sg, b=ups[:, 0:512]:
                             e.tensor_mul(o, a, b)),
                            [hsl], [sg, ups[:, 0:512]], tag="hmul")
                        if tt == 1 and ft == 0:
                            emit_down(wb_, base_d, 12288, 0, e_i, fb)
                    if tt == 1:
                        emit_down(wb_, base_d, hb, 1, e_i, fb)
        if th == 0:
            tap("acc", accf[:, :])
        for tt in range(2):
            t0 = tt * 512
            ln_stats(lambda c, t0=t0: accf[:, c * 1024 + t0:c * 1024 + t0 + 512],
                     F32, t0, 512, sqscr=(fs[:, 0:512], fs[:, 512:1024]))
            ln_res_add(lambda c, t0=t0: accf[:, c * 1024 + t0:
                                             c * 1024 + t0 + 512],
                       F32,
                       lambda c, t0=t0, th0=th0: xfc(c, th0 + t0, 512),
                       t0, 512, fs[:, 1024:1536], fs[:, 1536:2048], None)

    for ti in range(16):
        stg = stage if ti % 2 == 0 else stage2
        for c in range(NCH):
            pst = ps4 if c % 2 == 0 else ps6
            tp(pst[:, 0:128], xfc(c, ti * 128, 128), idf[:, :], tag="ot_tp")
            vop((lambda e, o=stg[:, c * 128:(c + 1) * 128], i=pst[:, 0:128]:
                 e.tensor_copy(o, i)),
                [stg[:, c * 128:(c + 1) * 128]], [pst[:, 0:128]], tag="ot_ev")
        dma("sync", Y[ti * 128:(ti + 1) * 128, :], stg[:, :], tag="out")

    from contextlib import ExitStack
    with ExitStack() as ctx:
        for k in SEM_KEYS:
            sems[k] = ctx.enter_context(nc.semaphore(k))
        block = ctx.enter_context(nc.Block())
        em.build(block, sems, strict=strict)
    return nc, em


def kernel(L0_attn_hidden_states, L0_attn_Wq, L0_attn_Wk, L0_attn_Wv,
           L0_attn_Wo, L0_attn_Aq, L0_attn_Bq, L0_attn_Av, L0_attn_Bv,
           L0_attn_Ao, L0_attn_Bo, L0_ffn_Wg, L0_ffn_Wu, L0_ffn_Wd):
    x = np.ascontiguousarray(np.asarray(L0_attn_hidden_states, dtype=np.float32))
    consts = make_consts()
    shared = dict(
        wq=np.ascontiguousarray(np.asarray(L0_attn_Wq, np.float32)),
        wk=np.ascontiguousarray(np.asarray(L0_attn_Wk, np.float32)),
        wv=np.ascontiguousarray(np.asarray(L0_attn_Wv, np.float32)),
        wo=np.ascontiguousarray(np.asarray(L0_attn_Wo, np.float32)),
        aq=np.ascontiguousarray(np.asarray(L0_attn_Aq, np.float32)),
        bq=np.ascontiguousarray(np.asarray(L0_attn_Bq, np.float32)),
        av=np.ascontiguousarray(np.asarray(L0_attn_Av, np.float32)),
        bv=np.ascontiguousarray(np.asarray(L0_attn_Bv, np.float32)),
        ao=np.ascontiguousarray(np.asarray(L0_attn_Ao, np.float32)),
        bo=np.ascontiguousarray(np.asarray(L0_attn_Bo, np.float32)),
        wg=np.ascontiguousarray(np.asarray(L0_ffn_Wg, np.float32)),
        wu=np.ascontiguousarray(np.asarray(L0_ffn_Wu, np.float32)),
        wd=np.ascontiguousarray(np.asarray(L0_ffn_Wd, np.float32)),
        **consts,
    )
    nc, _ = build_program(strict=("vector", "scalar", "gpsimd"))
    in_maps = [dict(x=np.ascontiguousarray(x[b]), **shared) for b in range(8)]
    res = run_bass_kernel_spmd(nc, in_maps, core_ids=list(range(8)))
    out = np.stack([r["y"] for r in res.results], axis=0).astype(np.float32)
    return out
